# version 04
# baseline (speedup 1.0000x reference)
import numpy as np
from contextlib import ExitStack

import concourse.bass as bass
import concourse.mybir as mybir
from concourse.bass_utils import run_bass_kernel_spmd

"""Host-side input prep shared by dev scripts and kernel.py."""

_DENSE_LOCAL_W = 64
_DYADIC_ALL = [96, 128, 192, 256, 384, 512, 768, 1024, 1536]
OFFSETS = sorted(set(range(0, _DENSE_LOCAL_W + 1)) | set(_DYADIC_ALL))
WIN_OFF = set(range(0, 65)) | {96, 128, 192}
DYADIC = [256, 384, 512, 768, 1024, 1536]
H = 16
NEG = -1e30


def make_bwin(pos_bias):
    pb = {o: pos_bias[m] for m, o in enumerate(OFFSETS)}
    out = np.full((H, 3, 128, 128), NEG, dtype=np.float32)
    for c in range(3):
        base = (2 - c) * 128
        for j in range(128):
            for i in range(128):
                off = base + i - j
                if off in WIN_OFF:
                    out[:, c, j, i] = pb[off]
    return out


def make_bdyn(pos_bias):
    idx = {o: m for m, o in enumerate(OFFSETS)}
    v = np.empty((len(DYADIC), H), dtype=np.float32)
    for mi, o in enumerate(DYADIC):
        v[mi, :] = pos_bias[idx[o]]
    row = v.reshape(-1)
    return np.broadcast_to(row, (128, H * len(DYADIC))).copy()


def core_inputs(ins, n_cores=8):
    x = np.ascontiguousarray(ins["L0_attn_x"], dtype=np.float32)
    bwin = np.ascontiguousarray(make_bwin(np.asarray(ins["L0_attn_pos_bias"])))
    bdyn = np.ascontiguousarray(make_bdyn(np.asarray(ins["L0_attn_pos_bias"])))
    com = {
        "wg": np.ascontiguousarray(ins["L0_ffn_w_gate"], dtype=np.float32),
        "wu": np.ascontiguousarray(ins["L0_ffn_w_up"], dtype=np.float32),
        "wd": np.ascontiguousarray(ins["L0_ffn_w_down"], dtype=np.float32),
        "wqkv": np.ascontiguousarray(ins["L0_attn_Wqkv"], dtype=np.float32),
        "bqkvb": np.ascontiguousarray(
            np.broadcast_to(ins["L0_attn_bqkv"], (128, 3072))).astype(np.float32),
        "wgate": np.ascontiguousarray(ins["L0_attn_Wgate"], dtype=np.float32),
        "bgateb": np.ascontiguousarray(
            np.broadcast_to(ins["L0_attn_bgate"], (128, 1024))).astype(np.float32),
        "wout": np.ascontiguousarray(ins["L0_attn_Wout"], dtype=np.float32),
        "boutb": np.ascontiguousarray(
            np.broadcast_to(ins["L0_attn_bout"], (128, 1024))).astype(np.float32),
        "bwin": bwin,
        "bdyn": bdyn,
    }
    per = x.shape[0] // n_cores
    maps = []
    for c in range(n_cores):
        m = dict(com)
        m["x"] = np.ascontiguousarray(x[c * per:(c + 1) * per])
        maps.append(m)
    return maps


"""Kernel-builder helpers: per-engine semaphore bookkeeping + dependency tracking.

Compute instructions increment their engine's progress semaphore (+1 each).
DMA queues use R rotating semaphores with issue-gating: a DMA on rotation
slot j may only issue once the previous DMA on slot j completed, so each
slot's semaphore is serialized and every 16*m boundary is a deterministic
wait value (concurrent DMAs on ONE semaphore post their 16 increments
interleaved, making intermediate waits racy).
"""

NDMASEM = 6


class Buf:
    __slots__ = ("name", "writer", "readers")

    def __init__(self, name):
        self.name = name
        self.writer = None
        self.readers = []


class Eng:
    def __init__(self, kb, name, eng, sem, dma_sems):
        self.kb = kb
        self.name = name
        self.e = eng
        self.sem = sem
        self.dma_sems = dma_sems
        self.dma_counts = [0] * len(dma_sems)
        self.dma_idx = 0
        self.n = 0
        self.known = {}

    def _wait(self, token):
        sem, val = token
        if self.name == "pe" and sem is self.sem:
            return
        if self.known.get(id(sem), -1) >= val:
            return
        self.e.wait_ge(sem, val)
        self.known[id(sem)] = val


class KB:
    def __init__(self, nc, es):
        self.nc = nc
        mk = lambda nm: es.enter_context(nc.semaphore(nm))
        self.pe = Eng(self, "pe", nc.tensor, mk("sem_pe"), [])
        self.dve = Eng(self, "dve", nc.vector, mk("sem_dve"), [])
        self.act = Eng(self, "act", nc.scalar, mk("sem_act"),
                       [mk(f"dsc{i}") for i in range(NDMASEM)])
        self.pool = Eng(self, "pool", nc.gpsimd, mk("sem_pool"),
                        [mk(f"dg{i}") for i in range(NDMASEM)])
        self.sy = Eng(self, "sy", nc.sync, mk("sem_sy"),
                      [mk(f"dsy{i}") for i in range(NDMASEM)])
        self.engines = [self.pe, self.dve, self.act, self.pool, self.sy]

    def op(self, eng, method, *args, reads=(), writes=(), **kwargs):
        for b in reads:
            if b.writer is not None:
                eng._wait(b.writer)
        for b in writes:
            if b.writer is not None:
                eng._wait(b.writer)
            for t in b.readers:
                eng._wait(t)
        inst = getattr(eng.e, method)(*args, **kwargs)
        eng.n += 1
        inst.then_inc(eng.sem, 1)
        tok = (eng.sem, eng.n)
        for b in reads:
            b.readers.append(tok)
        for b in writes:
            b.writer = tok
            b.readers = []
        return tok

    def dma(self, eng, dst, src, reads=(), writes=(), transpose=False, **kw):
        for b in reads:
            if b.writer is not None:
                eng._wait(b.writer)
        for b in writes:
            if b.writer is not None:
                eng._wait(b.writer)
            for t in b.readers:
                eng._wait(t)
        j = eng.dma_idx % len(eng.dma_sems)
        eng.dma_idx += 1
        sem = eng.dma_sems[j]
        if eng.dma_counts[j] > 0:
            eng._wait((sem, 16 * eng.dma_counts[j]))
        if transpose:
            inst = eng.e.dma_start_transpose(dst, src)
        else:
            inst = eng.e.dma_start(dst, src, **kw)
        eng.dma_counts[j] += 1
        inst.then_inc(sem, 16)
        tok = (sem, 16 * eng.dma_counts[j])
        for b in reads:
            b.readers.append(tok)
        for b in writes:
            b.writer = tok
            b.readers = []
        return tok

    def barrier_token(self, eng):
        return (eng.sem, eng.n)

    def finalize(self):
        for e in self.engines:
            for j, sem in enumerate(e.dma_sems):
                if e.dma_counts[j] > 0:
                    e._wait((sem, 16 * e.dma_counts[j]))


def full_barrier(kb):
    toks = []
    for e in kb.engines:
        if e.n > 0:
            toks.append((e.sem, e.n))
        for j, sem in enumerate(e.dma_sems):
            if e.dma_counts[j] > 0:
                toks.append((sem, 16 * e.dma_counts[j]))
    for e in kb.engines:
        for t in toks:
            e._wait(t)


"""Megakernel builder for mk_prb_d1: out = x + attn(LN(x)) + ffn(LN(x)).

Per core: n_seq sequences of 2048 tokens. Two weight-phased passes inside ONE
program: pass1 = FFN (fp4 SwiGLU) writing partial = x + ffn + bout to DRAM
scratch (plus xn bf16 cache); pass2 = sparse attention reading partial/xn and
writing out.
"""

F32 = mybir.dt.float32
BF16 = mybir.dt.bfloat16
F16 = mybir.dt.float16
ALU = mybir.AluOpType
AX = mybir.AxisListType
AF = mybir.ActivationFunctionType

D = 1024
N = 2048
I = 2816
ICHUNKS = [(0, 512), (512, 512), (1024, 512), (1536, 512), (2048, 512), (2560, 256)]
H = 16
HD = 64
WCH = 3
DYADIC = [256, 384, 512, 768, 1024, 1536]
ND = len(DYADIC)
EPS = 1e-5
QW = 768


def emit_ln(kb, t, b, x_ap, xnb_ap, xnf_ap, bx, bxnb, bxnf):
    dve = kb.dve
    kb.op(dve, "reduce_sum", t["s1"][:], x_ap, AX.X, reads=[bx], writes=[b["s1"]])
    kb.op(dve, "tensor_mul", t["xsq"][:], x_ap, x_ap, reads=[bx], writes=[b["xsq"]])
    kb.op(dve, "reduce_sum", t["s2"][:], t["xsq"][:], AX.X, reads=[b["xsq"]], writes=[b["s2"]])
    kb.op(dve, "tensor_scalar", t["mean"][:], t["s1"][:], 1.0 / D, None, ALU.mult,
          reads=[b["s1"]], writes=[b["mean"]])
    kb.op(dve, "tensor_mul", t["tmp1"][:], t["mean"][:], t["mean"][:],
          reads=[b["mean"]], writes=[b["tmp1"]])
    kb.op(dve, "scalar_tensor_tensor", t["var"][:], t["s2"][:], 1.0 / D, t["tmp1"][:],
          ALU.mult, ALU.subtract, reads=[b["s2"], b["tmp1"]], writes=[b["var"]])
    kb.op(dve, "tensor_scalar", t["var"][:], t["var"][:], 1.0, EPS, ALU.mult, ALU.add,
          reads=[b["var"]], writes=[b["var"]])
    kb.op(dve, "tensor_scalar", t["y"][:], t["var"][:], -0.5, 1.5, ALU.mult, ALU.add,
          reads=[b["var"]], writes=[b["y"]])
    for _ in range(2):
        kb.op(dve, "tensor_mul", t["tmp1"][:], t["y"][:], t["y"][:],
              reads=[b["y"]], writes=[b["tmp1"]])
        kb.op(dve, "tensor_mul", t["tmp1"][:], t["tmp1"][:], t["var"][:],
              reads=[b["tmp1"], b["var"]], writes=[b["tmp1"]])
        kb.op(dve, "tensor_scalar", t["tmp1"][:], t["tmp1"][:], -0.5, 1.5, ALU.mult, ALU.add,
              reads=[b["tmp1"]], writes=[b["tmp1"]])
        kb.op(dve, "tensor_mul", t["y"][:], t["y"][:], t["tmp1"][:],
              reads=[b["y"], b["tmp1"]], writes=[b["y"]])
    kb.op(dve, "scalar_tensor_tensor", t["nmr"][:], t["mean"][:], -1.0, t["y"][:],
          ALU.mult, ALU.mult, reads=[b["mean"], b["y"]], writes=[b["nmr"]])
    kb.op(dve, "tensor_scalar", xnf_ap, x_ap, t["y"][:], t["nmr"][:], ALU.mult, ALU.add,
          reads=[bx, b["y"], b["nmr"]], writes=[bxnf])
    kb.op(dve, "tensor_copy", xnb_ap, xnf_ap, reads=[bxnf], writes=[bxnb])


def emit_quant(kb, t, b, xs_ap, xq_ap, bxs, bxq, width, par=0):
    dve, act = kb.dve, kb.act
    nb = width // 16
    W = width
    I16 = mybir.dt.int16
    v3 = lambda ap: ap.rearrange("p (a c) -> p a c", c=16)
    xa, qa, qb_, qd = (t["qxa"][:, :W], t["qa"][:, :W], t["qb"][:, :W], t["qd"][:, :W])
    m, mc, r6f = t["qm"][:, :nb], t["qmc"][:, :nb], t["qr6f"][:, :nb]
    r6h, sbh = t["qr6h"][:, :nb], t["qsbh"][:, :nb]
    B = {k: b[f"q_{k}{par}"] for k in ("xa", "m", "mc", "r6f", "r6h", "sbh", "a", "b", "d")}
    B["mc"] = B["m"]
    kb.op(act, "activation", xa, xs_ap, AF.Abs, reads=[bxs], writes=[B["xa"]])
    kb.op(dve, "reduce_max", m, v3(xa), AX.X, reads=[B["xa"]], writes=[B["m"]])
    kb.op(dve, "tensor_scalar", mc, m, 1e-4, None, ALU.max, reads=[B["m"]], writes=[B["mc"]])
    kb.op(dve, "tensor_scalar", sbh, mc, 1.0 / 6.0, None, ALU.mult,
          reads=[B["mc"]], writes=[B["sbh"]])
    kb.op(dve, "reciprocal", r6f, mc, reads=[B["mc"]], writes=[B["r6f"]])
    kb.op(dve, "tensor_scalar", r6h, r6f, 6.0, None, ALU.mult,
          reads=[B["r6f"]], writes=[B["r6h"]])
    bc = r6h.unsqueeze(2).broadcast_to([128, nb, 16])
    kb.op(dve, "tensor_tensor", v3(xa), v3(xa), bc, ALU.mult,
          reads=[B["xa"], B["r6h"]], writes=[B["xa"]])
    u = xa
    kb.op(dve, "tensor_scalar", qa.bitcast(I16), u, 2.0, None, ALU.mult,
          reads=[B["xa"]], writes=[B["a"]])
    kb.op(dve, "tensor_scalar", qa, qa.bitcast(I16), 4.0, 0.5, ALU.min, ALU.mult,
          reads=[B["a"]], writes=[B["a"]])
    kb.op(dve, "tensor_scalar", qb_.bitcast(I16), u, 1.0, None, ALU.mult,
          reads=[B["xa"]], writes=[B["b"]])
    kb.op(dve, "tensor_scalar", qb_, qb_.bitcast(I16), 2.0, 4.0, ALU.max, ALU.min,
          reads=[B["b"]], writes=[B["b"]])
    kb.op(dve, "tensor_scalar", qd.bitcast(I16), u, 4000.0, -20002.0, ALU.mult, ALU.add,
          reads=[B["xa"]], writes=[B["d"]])
    kb.op(dve, "tensor_scalar", qd, qd.bitcast(I16), -2.0, 0.0, ALU.max, ALU.min,
          reads=[B["d"]], writes=[B["d"]])
    kb.op(dve, "tensor_add", qa, qa, qb_, reads=[B["a"], B["b"]], writes=[B["a"]])
    kb.op(dve, "tensor_add", qa, qa, qd, reads=[B["a"], B["d"]], writes=[B["a"]])
    kb.op(act, "activation", qd, xs_ap, AF.Sign, reads=[bxs], writes=[B["d"]])
    kb.op(dve, "tensor_mul", qa, qa, qd, reads=[B["a"], B["d"]], writes=[B["a"]])
    sb_bc = sbh.unsqueeze(2).broadcast_to([128, nb, 16])
    kb.op(dve, "tensor_tensor", v3(xq_ap), v3(qa), sb_bc, ALU.mult,
          reads=[B["a"], B["sbh"]], writes=[bxq])


def build(n_seq=2, nt=16, only_pass1=False, dbg=None, no_dyadic=False, no_c0=False):
    nc = bass.Bass("TRN2", target_bir_lowering=False)
    es = ExitStack()
    kb = KB(nc, es)
    sbt = lambda name, shape, dt=BF16: es.enter_context(nc.sbuf_tensor(name, shape, dt))

    x_d = nc.dram_tensor("x", [n_seq, N, D], F32, kind="ExternalInput").ap()
    out_d = nc.dram_tensor("out", [n_seq, N, D], F32, kind="ExternalOutput").ap()
    part_d = nc.dram_tensor("part", [n_seq, N, D], F32, kind="Internal").ap()
    xnb_d = nc.dram_tensor("xnbc", [n_seq, N, D], BF16, kind="Internal").ap()
    wg_d = nc.dram_tensor("wg", [I, D], F32, kind="ExternalInput").ap()
    wu_d = nc.dram_tensor("wu", [I, D], F32, kind="ExternalInput").ap()
    wd_d = nc.dram_tensor("wd", [D, I], F32, kind="ExternalInput").ap()
    wqkv_d = nc.dram_tensor("wqkv", [3 * D, D], F32, kind="ExternalInput").ap()
    bqkv_d = nc.dram_tensor("bqkvb", [128, 3 * D], F32, kind="ExternalInput").ap()
    wgt_d = nc.dram_tensor("wgate", [D, D], F32, kind="ExternalInput").ap()
    bgt_d = nc.dram_tensor("bgateb", [128, D], F32, kind="ExternalInput").ap()
    wo_d = nc.dram_tensor("wout", [D, D], F32, kind="ExternalInput").ap()
    bo_d = nc.dram_tensor("boutb", [128, D], F32, kind="ExternalInput").ap()
    bwin_d = nc.dram_tensor("bwin", [H, WCH, 128, 128], F32, kind="ExternalInput").ap()
    bdyn_d = nc.dram_tensor("bdyn", [128, H * ND], F32, kind="ExternalInput").ap()

    B = {}
    def buf(nm):
        if nm not in B:
            B[nm] = Buf(nm)
        return B[nm]

    xnb = sbt("xnb", [128, D])
    small = [{nm: sbt(f"ln_{nm}{p}", [128, 1], F32)
              for nm in ("s1", "s2", "mean", "var", "y", "nmr", "tmp1")} for p in range(2)]
    part_t = sbt("part_t", [128, D], F32)
    ln_b = [{nm: buf(f"{nm}{p}") for nm in ("s1", "s2", "mean", "var", "y", "nmr", "tmp1", "xsq")}
            for p in range(2)]
    for p in range(2):
        for nm in ("xa", "m", "mc", "r6f", "r6h", "sbh", "a", "b", "d"):
            buf(f"q_{nm}{p}")
    b_xnb, b_part = buf("xnb"), buf("part_t")

    pe, dve, act, pool, sy = kb.pe, kb.dve, kb.act, kb.pool, kb.sy

    with ExitStack() as es1:
        s1t = lambda name, shape, dt=BF16: es1.enter_context(nc.sbuf_tensor(name, shape, dt))
        ffnw = s1t("ffnw", [128, 3 * 8 * I])
        WGO, WUO, WDO = 0, 8 * I, 16 * I
        FST = 3 * 8 * I
        wgqT = ffnw[:, WGO:WGO + 8 * I]
        wuqT = ffnw[:, WUO:WUO + 8 * I]
        wdqT = ffnw[:, WDO:WDO + 22 * D]
        b_wgqT, b_wuqT, b_wdqT = buf("wgqT"), buf("wuqT"), buf("wdqT")
        x_t = [s1t(f"x_t{p}", [128, D], F32) for p in range(2)]
        xnf = [s1t(f"xnf{p}", [128, D], F32) for p in range(2)]
        b_x = [buf(f"x_t{p}") for p in range(2)]
        b_xnf = [buf(f"xnf{p}") for p in range(2)]
        for p in range(2):
            small[p]["xsq"] = xnf[p]
            ln_b[p]["xsq"] = b_xnf[p]
        q_t = []
        for p in range(2):
            d = {}
            for nm in ("qxa", "qa", "qb", "qd"):
                d[nm] = s1t(f"{nm}{p}", [128, QW], F16)
            for nm in ("qm", "qr6f"):
                d[nm] = s1t(f"{nm}{p}", [128, QW // 16], F32)
            d["qmc"] = d["qm"]
            for nm in ("qr6h", "qsbh"):
                d[nm] = s1t(f"{nm}{p}", [128, QW // 16], F16)
            q_t.append(d)
        xq = [s1t(f"xq{p}", [128, D]) for p in range(2)]
        xq_fm = [s1t(f"xq_fm{p}", [128, D]) for p in range(2)]
        hq = [s1t(f"hq{p}", [128, I]) for p in range(2)]
        hq_fm = [s1t(f"hq_fm{p}", [128, I]) for p in range(2)]
        hsg1 = s1t("h_sg", [128, 512], F16)
        h_sg = [hsg1, hsg1]
        h_gt = [s1t(f"h_gt{p}", [128, 512], F16) for p in range(2)]
        hup1 = s1t("h_up", [128, 512], F16)
        h_up = [hup1, hup1]
        bias_out = s1t("bias_out", [128, D])
        b_xq = [buf(f"xq{p}") for p in range(2)]
        b_xqfm = [buf(f"xq_fm{p}") for p in range(2)]
        b_hq = [buf(f"hq{p}") for p in range(2)]
        b_hqfm = [buf(f"hq_fm{p}") for p in range(2)]
        b_hsg1 = buf("h_sg")
        b_hsg = [b_hsg1, b_hsg1]
        b_hgt = [buf(f"h_gt{p}") for p in range(2)]
        b_hup1 = buf("h_up")
        b_hup = [b_hup1, b_hup1]
        b_bo = buf("bias_out")
        xnbp = [xnb, xnb]
        b_xnbp = [b_xnb, b_xnb]
        part_p = [part_t, s1t("part2", [128, D], F32)]
        b_partp = [b_part, buf("part2")]
        wstage = x_t[0]
        b_ws = b_x[0]
        kb.dma(sy, xnf[0][:], bo_d, writes=[b_xnf[0]])
        kb.op(act, "copy", bias_out[:], xnf[0][:], reads=[b_xnf[0]], writes=[b_bo])
        for w_d, wTO, bwT, ntile, tw, ost in (
            (wg_d, WGO, b_wgqT, 22, D, I), (wu_d, WUO, b_wuqT, 22, D, I),
            (wd_d, WDO, b_wdqT, 8, I, D),
        ):
            for it in range(ntile):
                wst, bst = x_t[it % 2], b_x[it % 2]
                hqt, bhqt = hq[it % 2], b_hq[it % 2]
                for poff in range(0, tw, QW):
                    cw = min(QW, tw - poff)
                    kb.dma(sy, wst[:, 0:cw], w_d[it * 128:(it + 1) * 128, poff:poff + cw],
                           writes=[bst])
                    emit_quant(kb, q_t[it % 2], B, wst[:, 0:cw], hqt[:, poff:poff + cw],
                               bst, bhqt, cw, par=it % 2)
                nblk = tw // 128
                dst = bass.AP(ffnw, wTO + it * 128,
                              [[FST, 128], [ost, nblk], [1, 128]])
                src = hqt[:, 0:tw].rearrange("p (c q) -> p c q", c=nblk)
                kb.dma(act, dst, src, transpose=True, reads=[bhqt], writes=[bwT])
        pg = [es1.enter_context(nc.psum_tensor(f"pg{p}", [128, 512], F32)) for p in range(2)]
        pu = [es1.enter_context(nc.psum_tensor(f"pu{p}", [128, 512], F32)) for p in range(2)]
        pd = [es1.enter_context(nc.psum_tensor(f"pd{p}", [128, D], F32)) for p in range(2)]
        b_pg = [buf(f"pg{p}") for p in range(2)]
        b_pu = [buf(f"pu{p}") for p in range(2)]
        b_pd = [buf(f"pd{p}") for p in range(2)]

        for it in range(nt):
            for sq in range(n_seq):
                tp = sq % 2
                row = slice(it * 128, (it + 1) * 128)
                xt, bx = x_t[tp], b_x[tp]
                xnfp, bxnf = xnf[tp], b_xnf[tp]
                xnbt, bxnbt = xnbp[tp], b_xnbp[tp]
                xqt, bxq = xq[tp], b_xq[tp]
                xqf, bxqf = xq_fm[tp], b_xqfm[tp]
                hqt, bhq = hq[tp], b_hq[tp]
                hqf, bhqf = hq_fm[tp], b_hqfm[tp]
                hfl_ap = hqf[:].bitcast(F16)
                ptt, bpt = part_p[tp], b_partp[tp]
                kb.dma(sy, xt[:], x_d[sq, row, :], writes=[bx])
                emit_ln(kb, small[tp], ln_b[tp], xt[:], xnbt[:], xnfp[:], bx, bxnbt, bxnf)
                kb.dma(sy, xnb_d[sq, row, :], xnbt[:], reads=[bxnbt], writes=[buf("xnb_d")])
                for qoff in range(0, D, QW):
                    cw = min(QW, D - qoff)
                    emit_quant(kb, q_t[tp], B, xnfp[:, qoff:qoff + cw],
                               xqt[:, qoff:qoff + cw], bxnf, bxq, cw, par=tp)
                kb.dma(act, xqf[:].rearrange("p (c q) -> p c q", c=8),
                       xqt[:].rearrange("p (c q) -> p c q", c=8),
                       transpose=True, reads=[bxq], writes=[bxqf])
                for ci, (off, cw) in enumerate(ICHUNKS):
                    cp = ci % 2
                    for dc in range(8):
                        lhsT = xqf[:, dc * 128:(dc + 1) * 128]
                        kb.op(pe, "matmul", pg[cp][:, 0:cw], lhsT,
                              wgqT[:, dc * I + off: dc * I + off + cw],
                              start=(dc == 0), stop=(dc == 7),
                              reads=[bxqf, b_wgqT], writes=[b_pg[cp]])
                        kb.op(pe, "matmul", pu[cp][:, 0:cw], lhsT,
                              wuqT[:, dc * I + off: dc * I + off + cw],
                              start=(dc == 0), stop=(dc == 7),
                              reads=[bxqf, b_wuqT], writes=[b_pu[cp]])
                    kb.op(act, "activation", h_sg[tp][:, 0:cw], pg[cp][:, 0:cw], AF.Tanh,
                          scale=0.5, reads=[b_pg[cp]], writes=[b_hsg[tp]])
                    kb.op(act, "copy", h_gt[tp][:, 0:cw], pg[cp][:, 0:cw],
                          reads=[b_pg[cp]], writes=[b_hgt[tp]])
                    kb.op(act, "copy", h_up[tp][:, 0:cw], pu[cp][:, 0:cw],
                          reads=[b_pu[cp]], writes=[b_hup[tp]])
                    kb.op(dve, "tensor_scalar", h_sg[tp][:, 0:cw], h_sg[tp][:, 0:cw], 0.5, 0.5,
                          ALU.mult, ALU.add, reads=[b_hsg[tp]], writes=[b_hsg[tp]])
                    kb.op(dve, "tensor_mul", h_gt[tp][:, 0:cw], h_gt[tp][:, 0:cw], h_up[tp][:, 0:cw],
                          reads=[b_hgt[tp], b_hup[tp]], writes=[b_hgt[tp]])
                    kb.op(dve, "tensor_mul", hfl_ap[:, off:off + cw], h_gt[tp][:, 0:cw],
                          h_sg[tp][:, 0:cw], reads=[b_hgt[tp], b_hsg[tp]], writes=[bhqf])
                for qoff in range(0, I, QW):
                    cw = min(QW, I - qoff)
                    emit_quant(kb, q_t[tp], B, hfl_ap[:, qoff:qoff + cw],
                               hqt[:, qoff:qoff + cw], bhqf, bhq, cw, par=tp)
                kb.dma(act, hqf[:].rearrange("p (c q) -> p c q", c=22),
                       hqt[:].rearrange("p (c q) -> p c q", c=22),
                       transpose=True, reads=[bhq], writes=[bhqf])
                for half in range(2):
                    for ic in range(22):
                        kb.op(pe, "matmul", pd[tp][:, half * 512:(half + 1) * 512],
                              hqf[:, ic * 128:(ic + 1) * 128],
                              wdqT[:, ic * D + half * 512: ic * D + (half + 1) * 512],
                              start=(ic == 0), stop=(ic == 21),
                              reads=[bhqf, b_wdqT], writes=[b_pd[tp]])
                kb.op(dve, "tensor_add", ptt[:], pd[tp][:], xt[:],
                      reads=[b_pd[tp], bx], writes=[bpt])
                kb.op(dve, "tensor_add", ptt[:], ptt[:], bias_out[:],
                      reads=[bpt, b_bo], writes=[bpt])
                dst = (out_d if only_pass1 else part_d)[sq, row, :]
                kb.dma(sy, dst, ptt[:], reads=[bpt],
                       writes=([] if only_pass1 else [buf("part_d")]))

    if only_pass1:
        kb.finalize()
        es.close()
        return nc

    env = dict(locals()); env['sbt'] = sbt; env['dbg'] = dbg
    env['no_dyadic'] = no_dyadic; env['no_c0'] = no_c0
    build_pass2(nc, es, kb, B, buf, env)
    kb.finalize()
    es.close()
    return nc


def build_pass2(nc, es, kb, B, buf, env):
    sbt = env["sbt"]
    small = env["small"]
    xnb, part_t = env["xnb"], env["part_t"]
    x_d, out_d, part_d, xnb_d = env["x_d"], env["out_d"], env["part_d"], env["xnb_d"]
    wqkv_d, bqkv_d = env["wqkv_d"], env["bqkv_d"]
    wgt_d, bgt_d, wo_d = env["wgt_d"], env["bgt_d"], env["wo_d"]
    bwin_d, bdyn_d = env["bwin_d"], env["bdyn_d"]
    n_seq, nt = env["n_seq"], env["nt"]
    no_dyadic = env.get("no_dyadic"); no_c0 = env.get("no_c0")
    pe, dve, act, pool, sy = kb.pe, kb.dve, kb.act, kb.pool, kb.sy
    b_partd, b_xnbd = buf("part_d"), buf("xnb_d")
    NEG = -1e30

    with ExitStack() as es2:
        s2t = lambda name, shape, dt=BF16: es2.enter_context(nc.sbuf_tensor(name, shape, dt))
        attnw = s2t("attnw", [128, 8 * 3 * D + 16 * D])
        ATST = 8 * 3 * D + 16 * D
        WQO, WGO2, WOO = 0, 8 * 3 * D, 8 * 3 * D + 8 * D
        wqkvT = attnw[:, WQO:WQO + 24 * D]
        wgT = attnw[:, WGO2:WGO2 + 8 * D]
        woT = attnw[:, WOO:WOO + 8 * D]
        b_wq, b_wg2, b_wo = buf("wqkvT"), buf("wgT"), buf("woT")
        k_tm = s2t("k_tm", [128, 16 * D])
        v_tm = s2t("v_tm", [128, 16 * D])
        kfm = s2t("kfm", [128, 3 * D])
        ebw = s2t("ebw", [128, H * WCH * 128])
        ebd = s2t("ebd", [128, H * ND])
        bias_qkv = s2t("bias_qkv", [128, 3 * D])
        bias_gate = s2t("bias_gate", [128, D])
        ones_col = s2t("ones_col", [128, 1])
        qtm1 = s2t("q_tm", [128, D])
        q_tm = [qtm1, qtm1]
        q_fm = [s2t(f"q_fm{p}", [128, D]) for p in range(2)]
        e4 = [s2t(f"e4{p}", [128, 4 * WCH * 128]) for p in range(2)]
        sdyn = s2t("sdyn", [128, H * ND], F32)
        edyn = s2t("edyn", [128, H * ND])
        dsum = s2t("dsum", [128, H], F32)
        ssum = s2t("ssum", [128, H], F32)
        srec = [s2t(f"srec{p}", [128, H], F32) for p in range(2)]
        g1 = s2t("g1", [128, D])
        g21 = s2t("g2", [128, D])
        g2 = [g21, g21]
        g41 = s2t("g4", [128, D])
        g4 = [g41, g41]
        flat_sb = [s2t(f"flat_sb{p}", [128, D]) for p in range(2)]
        fg1t = s2t("fg", [128, D])
        fg = [fg1t, fg1t]
        fgf1 = s2t("fg_fm", [128, D])
        fg_fm = [fgf1, fgf1]
        xnb2p = [xnb, xnb]
        xnbfm2 = [s2t(f"xnbfm2{p}", [128, D]) for p in range(2)]
        part2p = [part_t, part_t]
        b_part2p = [B["part_t"], B["part_t"]]
        wstage2 = part_t
        b_ws2 = B["part_t"]
        b_ktm = [buf(f"k_tm{j}") for j in range(16)]
        b_vtm = [buf(f"v_tm{j}") for j in range(16)]
        b_kfm = [buf(f"kfm{j}") for j in range(3)]
        b_ebw, b_ebd = buf("ebw"), buf("ebd")
        b_bq, b_bg = buf("bias_qkv"), buf("bias_gate")
        b_ones = buf("ones_col")
        b_qtm1 = buf("q_tm")
        b_qtm = [b_qtm1, b_qtm1]
        b_qfm = [buf(f"q_fm{p}") for p in range(2)]
        b_e4 = [buf(f"e4{p}") for p in range(2)]
        b_sdyn, b_edyn = buf("sdyn"), buf("edyn")
        b_dsum, b_ssum = buf("dsum"), buf("ssum")
        b_srec = [buf(f"srec{p}") for p in range(2)]
        b_g1 = buf("g1")
        b_g21 = buf("g2")
        b_g2 = [b_g21, b_g21]
        b_g41 = buf("g4")
        b_g4 = [b_g41, b_g41]
        b_flat = [buf(f"flat_sb{p}") for p in range(2)]
        b_fg1 = buf("fg")
        b_fg = [b_fg1, b_fg1]
        b_fgf1 = buf("fg_fm")
        b_fgfm = [b_fgf1, b_fgf1]
        b_xnb2p = [B["xnb"], B["xnb"]]
        b_xnbfm2 = [buf(f"xnbfm2{p}") for p in range(2)]

        P1 = es2.enter_context(nc.psum_tensor("P1", [128, 2048], F32))
        P2 = es2.enter_context(nc.psum_tensor("P2", [128, D], F32))
        P3 = es2.enter_context(nc.psum_tensor("P3", [128, 512], F32))
        P4 = es2.enter_context(nc.psum_tensor("P4", [128, 512], F32))
        b_P1b = [buf(f"P1b{b}") for b in range(4)]
        b_P2, b_P3, b_P4 = buf("P2"), buf("P3"), buf("P4")
        qkv_slots = [(P1[:, 0:512], b_P1b[0]), (P1[:, 512:1024], b_P1b[1]),
                     (P4[:, 0:512], b_P4)]

        kb.op(dve, "memset", ones_col[:], 1.0, writes=[b_ones])
        kb.op(dve, "memset", P1[:], 0.0, writes=b_P1b)
        kb.op(dve, "memset", P2[:], 0.0, writes=[b_P2])
        kb.op(dve, "memset", P3[:], 0.0, writes=[b_P3])
        kb.op(dve, "memset", P4[:], 0.0, writes=[b_P4])
        for it in range(24):
            kb.dma(sy, wstage2[:], wqkv_d[it * 128:(it + 1) * 128, :], writes=[b_ws2])
            scale = 0.125 if 8 <= it < 16 else 1.0
            kb.op(act, "activation", xnb[:], wstage2[:], AF.Identity, scale=scale,
                  reads=[b_ws2], writes=[B["xnb"]])
            dst = bass.AP(attnw, WQO + it * 128, [[ATST, 128], [3 * D, 8], [1, 128]])
            kb.dma(act, dst, xnb[:].rearrange("p (c q) -> p c q", c=8),
                   transpose=True, reads=[B["xnb"]], writes=[b_wq])
        for w_d, base, bw in ((wgt_d, WGO2, b_wg2), (wo_d, WOO, b_wo)):
            for it in range(8):
                kb.dma(sy, wstage2[:], w_d[it * 128:(it + 1) * 128, :], writes=[b_ws2])
                kb.op(act, "copy", xnb[:], wstage2[:], reads=[b_ws2], writes=[B["xnb"]])
                dst = bass.AP(attnw, base + it * 128, [[ATST, 128], [D, 8], [1, 128]])
                kb.dma(act, dst, xnb[:].rearrange("p (c q) -> p c q", c=8),
                       transpose=True, reads=[B["xnb"]], writes=[bw])
        for c in range(3):
            kb.dma(sy, wstage2[:, 0:D], bqkv_d[:, c * D:(c + 1) * D], writes=[b_ws2])
            kb.op(act, "activation", bias_qkv[:, c * D:(c + 1) * D], wstage2[:, 0:D],
                  AF.Identity, scale=(0.125 if c == 1 else 1.0),
                  reads=[b_ws2], writes=[b_bq])
        kb.dma(sy, wstage2[:, 0:D], bgt_d, writes=[b_ws2])
        kb.op(act, "copy", bias_gate[:], wstage2[:, 0:D], reads=[b_ws2], writes=[b_bg])
        for h in range(H):
            for c in range(WCH):
                kb.dma(sy, wstage2[:, 0:128], bwin_d[h, c], writes=[b_ws2])
                kb.op(act, "activation", ebw[:, (h * WCH + c) * 128:(h * WCH + c + 1) * 128],
                      wstage2[:, 0:128], AF.Exp, reads=[b_ws2], writes=[b_ebw])
        kb.dma(sy, wstage2[:, 0:H * ND], bdyn_d, writes=[b_ws2])
        kb.op(act, "activation", ebd[:], wstage2[:, 0:H * ND], AF.Exp,
              reads=[b_ws2], writes=[b_ebd])

        for s in range(n_seq):
            for it in range(nt):
                tp = it % 2
                row = slice(it * 128, (it + 1) * 128)
                ptt, bpt = part2p[tp], b_part2p[tp]
                xnbt, bxnbt = xnb2p[tp], b_xnb2p[tp]
                xfm, bxfm = xnbfm2[tp], b_xnbfm2[tp]
                qtm, bqtm = q_tm[tp], b_qtm[tp]
                qfm, bqfm = q_fm[tp], b_qfm[tp]
                kb.dma(sy, ptt[:], part_d[s, row, :], reads=[b_partd], writes=[bpt])
                kb.dma(sy, xnbt[:], xnb_d[s, row, :], reads=[b_xnbd], writes=[bxnbt])
                kb.dma(act, xfm[:].rearrange("p (c q) -> p c q", c=8),
                       xnbt[:].rearrange("p (c q) -> p c q", c=8),
                       transpose=True, reads=[bxnbt], writes=[bxfm])
                for g in range(6):
                    ps_g, b_ps = qkv_slots[g % 3]
                    for dc in range(8):
                        kb.op(pe, "matmul", ps_g, xfm[:, dc * 128:(dc + 1) * 128],
                              wqkvT[:, dc * 3 * D + g * 512: dc * 3 * D + (g + 1) * 512],
                              start=(dc == 0), stop=(dc == 7),
                              reads=[bxfm, b_wq], writes=[b_ps])
                    if g < 2:
                        dstap, dbuf = qtm[:, g * 512:(g + 1) * 512], bqtm
                    elif g < 4:
                        dstap = k_tm[:, it * D + (g - 2) * 512: it * D + (g - 1) * 512]
                        dbuf = b_ktm[it]
                    else:
                        dstap = v_tm[:, it * D + (g - 4) * 512: it * D + (g - 3) * 512]
                        dbuf = b_vtm[it]
                    kb.op(dve, "scalar_tensor_tensor", dstap, ps_g, 1.0,
                          bias_qkv[:, g * 512:(g + 1) * 512], ALU.mult, ALU.add,
                          reads=[b_ps, b_bq], writes=[dbuf])
                kb.dma(act, qfm[:].rearrange("p (c q) -> p c q", c=8),
                       qtm[:].rearrange("p (c q) -> p c q", c=8),
                       transpose=True, reads=[bqtm], writes=[bqfm])
                kb.dma(act, kfm[:, (it % 3) * D:(it % 3 + 1) * D].rearrange("p (c q) -> p c q", c=8),
                       k_tm[:, it * D:(it + 1) * D].rearrange("p (c q) -> p c q", c=8),
                       transpose=True, reads=[b_ktm[it]], writes=[b_kfm[it % 3]])
                kb.op(dve, "memset", sdyn[:], NEG, writes=[b_sdyn])
                for mi, o in enumerate(DYADIC):
                    ot = o // 128
                    if it < ot or no_dyadic:
                        continue
                    kb.op(dve, "tensor_mul", g1[:], qtm[:],
                          k_tm[:, (it - ot) * D:(it - ot + 1) * D],
                          reads=[bqtm, b_ktm[it - ot]], writes=[b_g1])
                    kb.op(dve, "reduce_sum", sdyn[:, mi * H:(mi + 1) * H],
                          g1[:].rearrange("p (h d) -> p h d", d=HD), AX.X,
                          reads=[b_g1], writes=[b_sdyn])
                kb.op(act, "activation", edyn[:], sdyn[:], AF.Exp,
                      reads=[b_sdyn], writes=[b_edyn])
                kb.op(dve, "tensor_mul", edyn[:], edyn[:], ebd[:],
                      reads=[b_edyn, b_ebd], writes=[b_edyn])
                ev = bass.AP(edyn, 0, [[edyn[:].ap[0][0], 128], [1, H], [H, ND]])
                kb.op(dve, "reduce_sum", dsum[:], ev, AX.X,
                      reads=[b_edyn], writes=[b_dsum])
                for hg in range(4):
                    ep = hg % 2
                    heads = range(4 * hg, 4 * hg + 4)
                    for hh, h in sorted(enumerate(heads), key=lambda p: p[1] % 2):
                        for c in range(WCH):
                            jt = it - 2 + c
                            reg = P1[:, hh * 512 + c * 128: hh * 512 + (c + 1) * 128]
                            if jt < 0 or (no_c0 and c == 0):
                                kb.op(dve, "memset", reg, NEG, writes=[b_P1b[hh]])
                                continue
                            pb = (h % 2) * 64
                            lhsT = kfm[pb:pb + 64,
                                       (jt % 3) * D + (h // 2) * 128:(jt % 3) * D + (h // 2) * 128 + 128]
                            rhs = qfm[pb:pb + 64, (h // 2) * 128:(h // 2) * 128 + 128]
                            kb.op(pe, "matmul", reg, lhsT, rhs, start=True, stop=True,
                                  reads=[b_kfm[jt % 3], bqfm], writes=[b_P1b[hh]])
                    pv = P1[:].rearrange("p (a r) -> p a r", r=512)[:, :, 0:WCH * 128]
                    ev4 = e4[ep][:].rearrange("p (a r) -> p a r", r=WCH * 128)
                    kb.op(act, "activation", ev4, pv, AF.Exp,
                          reads=b_P1b, writes=[b_e4[ep]])
                    ebv = ebw[:, (4 * hg) * WCH * 128:(4 * hg + 4) * WCH * 128]
                    kb.op(dve, "tensor_mul", e4[ep][:], e4[ep][:], ebv,
                          reads=[b_e4[ep], b_ebw], writes=[b_e4[ep]])
                    for hh, h in enumerate(heads):
                        valid = [c for c in range(WCH)
                                 if it - 2 + c >= 0 and not (no_c0 and c == 0)]
                        for vi, c in enumerate(valid):
                            jt = it - 2 + c
                            ech = e4[ep][:, hh * 384 + c * 128: hh * 384 + (c + 1) * 128]
                            kb.op(pe, "matmul", P3[:, h:h + 1], ech, ones_col[:],
                                  start=(vi == 0), stop=(vi == len(valid) - 1),
                                  skip_group_check=True,
                                  reads=[b_e4[ep], b_ones], writes=[b_P3])
                            kb.op(pe, "matmul", P2[:, h * HD:(h + 1) * HD], ech,
                                  v_tm[:, jt * D + h * HD: jt * D + (h + 1) * HD],
                                  start=(vi == 0), stop=(vi == len(valid) - 1),
                                  skip_group_check=True,
                                  reads=[b_e4[ep], b_vtm[jt]], writes=[b_P2])
                kb.op(dve, "tensor_add", ssum[:], P3[:, 0:H], dsum[:],
                      reads=[b_P3, b_dsum], writes=[b_ssum])
                kb.op(dve, "reciprocal", srec[tp][:], ssum[:],
                      reads=[b_ssum], writes=[b_srec[tp]])
                vd = [] if no_dyadic else [o for o in DYADIC if it >= o // 128]
                if not vd:
                    kb.op(dve, "memset", g2[tp][:], 0.0, writes=[b_g2[tp]])
                for vi, o in enumerate(vd):
                    mi = DYADIC.index(o)
                    e_b = edyn[:, mi * H:(mi + 1) * H]
                    e_bc = e_b.unsqueeze(2).broadcast_to([128, H, HD])
                    tgt, tb = (g2[tp], b_g2[tp]) if vi == 0 else (g1, b_g1)
                    kb.op(dve, "tensor_tensor",
                          tgt[:].rearrange("p (h d) -> p h d", d=HD),
                          v_tm[:, (it - o // 128) * D:(it - o // 128 + 1) * D]
                          .rearrange("p (h d) -> p h d", d=HD),
                          e_bc, ALU.mult,
                          reads=[b_vtm[it - o // 128], b_edyn], writes=[tb])
                    if vi > 0:
                        kb.op(dve, "tensor_add", g2[tp][:], g2[tp][:], g1[:],
                              reads=[b_g2[tp], b_g1], writes=[b_g2[tp]])
                srec_bc = srec[tp][:].unsqueeze(2).broadcast_to([128, H, HD])
                v3h = lambda ap: ap.rearrange("p (h d) -> p h d", d=HD)
                kb.op(dve, "tensor_tensor", v3h(flat_sb[tp][:]), v3h(P2[:]), srec_bc,
                      ALU.mult, reads=[b_P2, b_srec[tp]], writes=[b_flat[tp]])
                kb.op(dve, "tensor_tensor", v3h(g2[tp][:]), v3h(g2[tp][:]), srec_bc,
                      ALU.mult, reads=[b_g2[tp], b_srec[tp]], writes=[b_g2[tp]])
                kb.op(dve, "tensor_add", flat_sb[tp][:], flat_sb[tp][:], g2[tp][:],
                      reads=[b_flat[tp], b_g2[tp]], writes=[b_flat[tp]])
                for g in range(2):
                    for dc in range(8):
                        kb.op(pe, "matmul", P2[:, g * 512:(g + 1) * 512],
                              xfm[:, dc * 128:(dc + 1) * 128],
                              wgT[:, dc * D + g * 512: dc * D + (g + 1) * 512],
                              start=(dc == 0), stop=(dc == 7),
                              reads=[bxfm, b_wg2], writes=[b_P2])
                kb.op(dve, "tensor_add", g4[tp][:], P2[:, 0:D], bias_gate[:],
                      reads=[b_P2, b_bg], writes=[b_g4[tp]])
                kb.op(act, "activation", g4[tp][:], g4[tp][:], AF.Tanh, scale=0.5,
                      reads=[b_g4[tp]], writes=[b_g4[tp]])
                kb.op(dve, "tensor_scalar", g4[tp][:], g4[tp][:], 0.5, 0.5,
                      ALU.mult, ALU.add, reads=[b_g4[tp]], writes=[b_g4[tp]])
                kb.op(dve, "tensor_mul", fg[tp][:], flat_sb[tp][:], g4[tp][:],
                      reads=[b_flat[tp], b_g4[tp]], writes=[b_fg[tp]])
                kb.dma(act, fg_fm[tp][:].rearrange("p (c q) -> p c q", c=8),
                       fg[tp][:].rearrange("p (c q) -> p c q", c=8),
                       transpose=True, reads=[b_fg[tp]], writes=[b_fgfm[tp]])
                for g in range(2):
                    for dc in range(8):
                        kb.op(pe, "matmul", P2[:, g * 512:(g + 1) * 512],
                              fg_fm[tp][:, dc * 128:(dc + 1) * 128],
                              woT[:, dc * D + g * 512: dc * D + (g + 1) * 512],
                              start=(dc == 0), stop=(dc == 7),
                              reads=[b_fgfm[tp], b_wo], writes=[b_P2])
                kb.op(dve, "tensor_add", ptt[:], P2[:], ptt[:],
                      reads=[b_P2, bpt], writes=[bpt])
                kb.dma(sy, out_d[s, row, :], ptt[:], reads=[bpt])


_NC_CACHE = {}


def kernel(**inputs):
    maps = core_inputs(inputs, n_cores=8)
    if "nc" not in _NC_CACHE:
        _NC_CACHE["nc"] = build(n_seq=2, nt=16, only_pass1=False)
    nc = _NC_CACHE["nc"]
    res = run_bass_kernel_spmd(nc, maps, core_ids=list(range(8)))
    out = np.concatenate([np.asarray(r["out"]) for r in res.results], axis=0)
    return out.astype(np.float32)


# version 05
# speedup vs baseline: 1.1217x; 1.1217x over previous
import numpy as np
from contextlib import ExitStack

import concourse.bass as bass
import concourse.mybir as mybir
from concourse.bass_utils import run_bass_kernel_spmd

"""Host-side input prep shared by dev scripts and kernel.py."""

_DENSE_LOCAL_W = 64
_DYADIC_ALL = [96, 128, 192, 256, 384, 512, 768, 1024, 1536]
OFFSETS = sorted(set(range(0, _DENSE_LOCAL_W + 1)) | set(_DYADIC_ALL))
WIN_OFF = set(range(0, 65)) | {96, 128, 192}
DYADIC = [256, 384, 512, 768, 1024, 1536]
H = 16
NEG = -1e30


def make_bwin(pos_bias):
    pb = {o: pos_bias[m] for m, o in enumerate(OFFSETS)}
    out = np.full((H, 3, 128, 128), NEG, dtype=np.float32)
    for c in range(3):
        base = (2 - c) * 128
        for j in range(128):
            for i in range(128):
                off = base + i - j
                if off in WIN_OFF:
                    out[:, c, j, i] = pb[off]
    return out


def make_bdyn(pos_bias):
    idx = {o: m for m, o in enumerate(OFFSETS)}
    v = np.empty((len(DYADIC), H), dtype=np.float32)
    for mi, o in enumerate(DYADIC):
        v[mi, :] = pos_bias[idx[o]]
    row = v.reshape(-1)
    return np.broadcast_to(row, (128, H * len(DYADIC))).copy()


def core_inputs(ins, n_cores=8):
    x = np.ascontiguousarray(ins["L0_attn_x"], dtype=np.float32)
    bwin = np.ascontiguousarray(make_bwin(np.asarray(ins["L0_attn_pos_bias"])))
    bdyn = np.ascontiguousarray(make_bdyn(np.asarray(ins["L0_attn_pos_bias"])))
    com = {
        "wg": np.ascontiguousarray(ins["L0_ffn_w_gate"], dtype=np.float32),
        "wu": np.ascontiguousarray(ins["L0_ffn_w_up"], dtype=np.float32),
        "wd": np.ascontiguousarray(ins["L0_ffn_w_down"], dtype=np.float32),
        "wqkv": np.ascontiguousarray(ins["L0_attn_Wqkv"], dtype=np.float32),
        "bqkvb": np.ascontiguousarray(
            np.broadcast_to(ins["L0_attn_bqkv"], (128, 3072))).astype(np.float32),
        "wgate": np.ascontiguousarray(ins["L0_attn_Wgate"], dtype=np.float32),
        "bgateb": np.ascontiguousarray(
            np.broadcast_to(ins["L0_attn_bgate"], (128, 1024))).astype(np.float32),
        "wout": np.ascontiguousarray(ins["L0_attn_Wout"], dtype=np.float32),
        "boutb": np.ascontiguousarray(
            np.broadcast_to(ins["L0_attn_bout"], (128, 1024))).astype(np.float32),
        "bwin": bwin,
        "bdyn": bdyn,
    }
    per = x.shape[0] // n_cores
    maps = []
    for c in range(n_cores):
        m = dict(com)
        m["x"] = np.ascontiguousarray(x[c * per:(c + 1) * per])
        maps.append(m)
    return maps


"""Kernel-builder helpers: per-engine semaphore bookkeeping + dependency tracking.

Compute instructions increment their engine's progress semaphore (+1 each).
DMA queues use R rotating semaphores with issue-gating: a DMA on rotation
slot j may only issue once the previous DMA on slot j completed, so each
slot's semaphore is serialized and every 16*m boundary is a deterministic
wait value (concurrent DMAs on ONE semaphore post their 16 increments
interleaved, making intermediate waits racy).
"""

NDMASEM = 6


class Buf:
    __slots__ = ("name", "writer", "readers")

    def __init__(self, name):
        self.name = name
        self.writer = None
        self.readers = []


class Eng:
    def __init__(self, kb, name, eng, sem, dma_sems):
        self.kb = kb
        self.name = name
        self.e = eng
        self.sem = sem
        self.dma_sems = dma_sems
        self.dma_counts = [0] * len(dma_sems)
        self.dma_idx = 0
        self.n = 0
        self.known = {}

    def _wait(self, token):
        sem, val = token
        if self.name == "pe" and sem is self.sem:
            return
        if self.known.get(id(sem), -1) >= val:
            return
        self.e.wait_ge(sem, val)
        self.known[id(sem)] = val


class KB:
    def __init__(self, nc, es):
        self.nc = nc
        mk = lambda nm: es.enter_context(nc.semaphore(nm))
        self.pe = Eng(self, "pe", nc.tensor, mk("sem_pe"), [])
        self.dve = Eng(self, "dve", nc.vector, mk("sem_dve"), [])
        self.act = Eng(self, "act", nc.scalar, mk("sem_act"),
                       [mk(f"dsc{i}") for i in range(NDMASEM)])
        self.pool = Eng(self, "pool", nc.gpsimd, mk("sem_pool"),
                        [mk(f"dg{i}") for i in range(NDMASEM)])
        self.sy = Eng(self, "sy", nc.sync, mk("sem_sy"),
                      [mk(f"dsy{i}") for i in range(NDMASEM)])
        self.engines = [self.pe, self.dve, self.act, self.pool, self.sy]

    def op(self, eng, method, *args, reads=(), writes=(), **kwargs):
        for b in reads:
            if b.writer is not None:
                eng._wait(b.writer)
        for b in writes:
            if b.writer is not None:
                eng._wait(b.writer)
            for t in b.readers:
                eng._wait(t)
        inst = getattr(eng.e, method)(*args, **kwargs)
        eng.n += 1
        inst.then_inc(eng.sem, 1)
        tok = (eng.sem, eng.n)
        for b in reads:
            b.readers.append(tok)
        for b in writes:
            b.writer = tok
            b.readers = []
        return tok

    def dma(self, eng, dst, src, reads=(), writes=(), transpose=False, **kw):
        for b in reads:
            if b.writer is not None:
                eng._wait(b.writer)
        for b in writes:
            if b.writer is not None:
                eng._wait(b.writer)
            for t in b.readers:
                eng._wait(t)
        j = eng.dma_idx % len(eng.dma_sems)
        eng.dma_idx += 1
        sem = eng.dma_sems[j]
        if eng.dma_counts[j] > 0:
            eng._wait((sem, 16 * eng.dma_counts[j]))
        if transpose:
            inst = eng.e.dma_start_transpose(dst, src)
        else:
            inst = eng.e.dma_start(dst, src, **kw)
        eng.dma_counts[j] += 1
        inst.then_inc(sem, 16)
        tok = (sem, 16 * eng.dma_counts[j])
        for b in reads:
            b.readers.append(tok)
        for b in writes:
            b.writer = tok
            b.readers = []
        return tok

    def barrier_token(self, eng):
        return (eng.sem, eng.n)

    def finalize(self):
        for e in self.engines:
            for j, sem in enumerate(e.dma_sems):
                if e.dma_counts[j] > 0:
                    e._wait((sem, 16 * e.dma_counts[j]))


def full_barrier(kb):
    toks = []
    for e in kb.engines:
        if e.n > 0:
            toks.append((e.sem, e.n))
        for j, sem in enumerate(e.dma_sems):
            if e.dma_counts[j] > 0:
                toks.append((sem, 16 * e.dma_counts[j]))
    for e in kb.engines:
        for t in toks:
            e._wait(t)


"""Megakernel builder for mk_prb_d1: out = x + attn(LN(x)) + ffn(LN(x)).

Per core: n_seq sequences of 2048 tokens. Two weight-phased passes inside ONE
program: pass1 = FFN (fp4 SwiGLU) writing partial = x + ffn + bout to DRAM
scratch (plus xn bf16 cache); pass2 = sparse attention reading partial/xn and
writing out.
"""

F32 = mybir.dt.float32
BF16 = mybir.dt.bfloat16
F16 = mybir.dt.float16
ALU = mybir.AluOpType
AX = mybir.AxisListType
AF = mybir.ActivationFunctionType

D = 1024
N = 2048
I = 2816
ICHUNKS = [(0, 512), (512, 512), (1024, 512), (1536, 512), (2048, 512), (2560, 256)]
H = 16
HD = 64
WCH = 3
DYADIC = [256, 384, 512, 768, 1024, 1536]
ND = len(DYADIC)
EPS = 1e-5
QW = 1408


def emit_ln(kb, t, b, x_ap, xnb_ap, xnf_ap, bx, bxnb, bxnf):
    dve = kb.dve
    kb.op(dve, "reduce_sum", t["s1"][:], x_ap, AX.X, reads=[bx], writes=[b["s1"]])
    kb.op(dve, "tensor_mul", t["xsq"][:], x_ap, x_ap, reads=[bx], writes=[b["xsq"]])
    kb.op(dve, "reduce_sum", t["s2"][:], t["xsq"][:], AX.X, reads=[b["xsq"]], writes=[b["s2"]])
    kb.op(dve, "tensor_scalar", t["mean"][:], t["s1"][:], 1.0 / D, None, ALU.mult,
          reads=[b["s1"]], writes=[b["mean"]])
    kb.op(dve, "tensor_mul", t["tmp1"][:], t["mean"][:], t["mean"][:],
          reads=[b["mean"]], writes=[b["tmp1"]])
    kb.op(dve, "scalar_tensor_tensor", t["var"][:], t["s2"][:], 1.0 / D, t["tmp1"][:],
          ALU.mult, ALU.subtract, reads=[b["s2"], b["tmp1"]], writes=[b["var"]])
    kb.op(dve, "tensor_scalar", t["var"][:], t["var"][:], 1.0, EPS, ALU.mult, ALU.add,
          reads=[b["var"]], writes=[b["var"]])
    kb.op(dve, "tensor_scalar", t["y"][:], t["var"][:], -0.5, 1.5, ALU.mult, ALU.add,
          reads=[b["var"]], writes=[b["y"]])
    for _ in range(2):
        kb.op(dve, "tensor_mul", t["tmp1"][:], t["y"][:], t["y"][:],
              reads=[b["y"]], writes=[b["tmp1"]])
        kb.op(dve, "tensor_mul", t["tmp1"][:], t["tmp1"][:], t["var"][:],
              reads=[b["tmp1"], b["var"]], writes=[b["tmp1"]])
        kb.op(dve, "tensor_scalar", t["tmp1"][:], t["tmp1"][:], -0.5, 1.5, ALU.mult, ALU.add,
              reads=[b["tmp1"]], writes=[b["tmp1"]])
        kb.op(dve, "tensor_mul", t["y"][:], t["y"][:], t["tmp1"][:],
              reads=[b["y"], b["tmp1"]], writes=[b["y"]])
    kb.op(dve, "scalar_tensor_tensor", t["nmr"][:], t["mean"][:], -1.0, t["y"][:],
          ALU.mult, ALU.mult, reads=[b["mean"], b["y"]], writes=[b["nmr"]])
    kb.op(dve, "tensor_scalar", xnf_ap, x_ap, t["y"][:], t["nmr"][:], ALU.mult, ALU.add,
          reads=[bx, b["y"], b["nmr"]], writes=[bxnf])


def emit_quant(kb, t, b, xs_ap, xq_ap, bxs, bxq, width, par=0):
    dve, act = kb.dve, kb.act
    nb = width // 16
    W = width
    I16 = mybir.dt.int16
    v3 = lambda ap: ap.rearrange("p (a c) -> p a c", c=16)
    xa, qa, qb_, qd = (t["qxa"][:, :W], t["qa"][:, :W], t["qb"][:, :W], t["qd"][:, :W])
    m, mc, r6f = t["qm"][:, :nb], t["qmc"][:, :nb], t["qr6f"][:, :nb]
    r6h, sbh = t["qr6h"][:, :nb], t["qsbh"][:, :nb]
    B = {k: b[f"q_{k}"] for k in ("m", "mc", "r6f", "r6h", "sbh", "a", "b", "d")}
    B["xa"] = b[f"q_xa{par}"]
    B["mc"] = B["m"]
    kb.op(act, "activation", xa, xs_ap, AF.Abs, reads=[bxs], writes=[B["xa"]])
    kb.op(dve, "reduce_max", m, v3(xa), AX.X, reads=[B["xa"]], writes=[B["m"]])
    kb.op(dve, "tensor_scalar", mc, m, 1e-4, None, ALU.max, reads=[B["m"]], writes=[B["mc"]])
    kb.op(dve, "tensor_scalar", sbh, mc, 1.0 / 6.0, None, ALU.mult,
          reads=[B["mc"]], writes=[B["sbh"]])
    kb.op(dve, "reciprocal", r6f, mc, reads=[B["mc"]], writes=[B["r6f"]])
    kb.op(dve, "tensor_scalar", r6h, r6f, 6.0, None, ALU.mult,
          reads=[B["r6f"]], writes=[B["r6h"]])
    bc = r6h.unsqueeze(2).broadcast_to([128, nb, 16])
    kb.op(dve, "tensor_tensor", v3(xa), v3(xa), bc, ALU.mult,
          reads=[B["xa"], B["r6h"]], writes=[B["xa"]])
    u = xa
    kb.op(dve, "tensor_scalar", qa.bitcast(I16), u, 2.0, None, ALU.mult,
          reads=[B["xa"]], writes=[B["a"]])
    kb.op(dve, "tensor_scalar", qa, qa.bitcast(I16), 4.0, 0.5, ALU.min, ALU.mult,
          reads=[B["a"]], writes=[B["a"]])
    kb.op(dve, "tensor_scalar", qb_.bitcast(I16), u, 1.0, None, ALU.mult,
          reads=[B["xa"]], writes=[B["b"]])
    kb.op(dve, "tensor_scalar", qb_, qb_.bitcast(I16), 2.0, 4.0, ALU.max, ALU.min,
          reads=[B["b"]], writes=[B["b"]])
    kb.op(dve, "tensor_scalar", qd.bitcast(I16), u, 4000.0, -20002.0, ALU.mult, ALU.add,
          reads=[B["xa"]], writes=[B["d"]])
    kb.op(dve, "tensor_scalar", qd, qd.bitcast(I16), -2.0, 0.0, ALU.max, ALU.min,
          reads=[B["d"]], writes=[B["d"]])
    kb.op(dve, "tensor_add", qa, qa, qb_, reads=[B["a"], B["b"]], writes=[B["a"]])
    kb.op(dve, "tensor_add", qa, qa, qd, reads=[B["a"], B["d"]], writes=[B["a"]])
    kb.op(act, "activation", qd, xs_ap, AF.Sign, reads=[bxs], writes=[B["d"]])
    kb.op(dve, "tensor_mul", qa, qa, qd, reads=[B["a"], B["d"]], writes=[B["a"]])
    sb_bc = sbh.unsqueeze(2).broadcast_to([128, nb, 16])
    kb.op(dve, "tensor_tensor", v3(xq_ap), v3(qa), sb_bc, ALU.mult,
          reads=[B["a"], B["sbh"]], writes=[bxq])


def build(n_seq=2, nt=16, only_pass1=False, dbg=None, no_dyadic=False, no_c0=False):
    nc = bass.Bass("TRN2", target_bir_lowering=False)
    es = ExitStack()
    kb = KB(nc, es)
    sbt = lambda name, shape, dt=BF16: es.enter_context(nc.sbuf_tensor(name, shape, dt))

    x_d = nc.dram_tensor("x", [n_seq, N, D], F32, kind="ExternalInput").ap()
    out_d = nc.dram_tensor("out", [n_seq, N, D], F32, kind="ExternalOutput").ap()
    part_d = nc.dram_tensor("part", [n_seq, N, D], F32, kind="Internal").ap()
    xnb_d = nc.dram_tensor("xnbc", [n_seq, N, D], BF16, kind="Internal").ap()
    wg_d = nc.dram_tensor("wg", [I, D], F32, kind="ExternalInput").ap()
    wu_d = nc.dram_tensor("wu", [I, D], F32, kind="ExternalInput").ap()
    wd_d = nc.dram_tensor("wd", [D, I], F32, kind="ExternalInput").ap()
    wqkv_d = nc.dram_tensor("wqkv", [3 * D, D], F32, kind="ExternalInput").ap()
    bqkv_d = nc.dram_tensor("bqkvb", [128, 3 * D], F32, kind="ExternalInput").ap()
    wgt_d = nc.dram_tensor("wgate", [D, D], F32, kind="ExternalInput").ap()
    bgt_d = nc.dram_tensor("bgateb", [128, D], F32, kind="ExternalInput").ap()
    wo_d = nc.dram_tensor("wout", [D, D], F32, kind="ExternalInput").ap()
    bo_d = nc.dram_tensor("boutb", [128, D], F32, kind="ExternalInput").ap()
    bwin_d = nc.dram_tensor("bwin", [H, WCH, 128, 128], F32, kind="ExternalInput").ap()
    bdyn_d = nc.dram_tensor("bdyn", [128, H * ND], F32, kind="ExternalInput").ap()

    B = {}
    def buf(nm):
        if nm not in B:
            B[nm] = Buf(nm)
        return B[nm]

    xnb = sbt("xnb", [128, D])
    small = [{nm: sbt(f"ln_{nm}{p}", [128, 1], F32)
              for nm in ("s1", "s2", "mean", "var", "y", "nmr", "tmp1")} for p in range(2)]
    part_t = sbt("part_t", [128, D], F32)
    ln_b = [{nm: buf(f"{nm}{p}") for nm in ("s1", "s2", "mean", "var", "y", "nmr", "tmp1", "xsq")}
            for p in range(2)]
    for p in range(2):
        buf(f"q_xa{p}")
    for nm in ("m", "mc", "r6f", "r6h", "sbh", "a", "b", "d"):
        buf(f"q_{nm}")
    b_xnb, b_part = buf("xnb"), buf("part_t")

    pe, dve, act, pool, sy = kb.pe, kb.dve, kb.act, kb.pool, kb.sy

    with ExitStack() as es1:
        s1t = lambda name, shape, dt=BF16: es1.enter_context(nc.sbuf_tensor(name, shape, dt))
        ffnw = s1t("ffnw", [128, 3 * 8 * I])
        WGO, WUO, WDO = 0, 8 * I, 16 * I
        FST = 3 * 8 * I
        wgqT = ffnw[:, WGO:WGO + 8 * I]
        wuqT = ffnw[:, WUO:WUO + 8 * I]
        wdqT = ffnw[:, WDO:WDO + 22 * D]
        b_wgqT, b_wuqT, b_wdqT = buf("wgqT"), buf("wuqT"), buf("wdqT")
        x_t = [s1t(f"x_t{p}", [128, D], F32) for p in range(2)]
        xnf = [s1t(f"xnf{p}", [128, D], F32) for p in range(2)]
        b_x = [buf(f"x_t{p}") for p in range(2)]
        b_xnf = [buf(f"xnf{p}") for p in range(2)]
        for p in range(2):
            small[p]["xsq"] = xnf[p]
            ln_b[p]["xsq"] = b_xnf[p]
        qshared = {}
        for nm in ("qa", "qb", "qd"):
            qshared[nm] = s1t(nm, [128, QW], F16)
        for nm in ("qm", "qr6f"):
            qshared[nm] = s1t(nm, [128, QW // 16], F32)
        qshared["qmc"] = qshared["qm"]
        for nm in ("qr6h", "qsbh"):
            qshared[nm] = s1t(nm, [128, QW // 16], F16)
        q_t = []
        for p in range(2):
            d = dict(qshared)
            d["qxa"] = s1t(f"qxa{p}", [128, QW], F16)
            q_t.append(d)
        xq1 = s1t("xq", [128, D])
        xq = [xq1, xq1]
        xq_fm = [s1t(f"xq_fm{p}", [128, D]) for p in range(2)]
        hq = [s1t(f"hq{p}", [128, I]) for p in range(2)]
        hq_fm = [s1t(f"hq_fm{p}", [128, I]) for p in range(2)]
        hsg1 = s1t("h_sg", [128, 512], F16)
        h_sg = [hsg1, hsg1]
        h_gt = [s1t(f"h_gt{p}", [128, 512], F16) for p in range(2)]
        hup1 = s1t("h_up", [128, 512], F16)
        h_up = [hup1, hup1]
        bias_out = s1t("bias_out", [128, D])
        b_xq1 = buf("xq")
        b_xq = [b_xq1, b_xq1]
        b_xqfm = [buf(f"xq_fm{p}") for p in range(2)]
        b_hq = [buf(f"hq{p}") for p in range(2)]
        b_hqfm = [buf(f"hq_fm{p}") for p in range(2)]
        b_hsg1 = buf("h_sg")
        b_hsg = [b_hsg1, b_hsg1]
        b_hgt = [buf(f"h_gt{p}") for p in range(2)]
        b_hup1 = buf("h_up")
        b_hup = [b_hup1, b_hup1]
        b_bo = buf("bias_out")
        xnbp = [xnb, xnb]
        b_xnbp = [b_xnb, b_xnb]
        part_p = [part_t, s1t("part2", [128, D], F32)]
        b_partp = [b_part, buf("part2")]
        wstage = x_t[0]
        b_ws = b_x[0]
        kb.dma(sy, xnf[0][:], bo_d, writes=[b_xnf[0]])
        kb.op(act, "copy", bias_out[:], xnf[0][:], reads=[b_xnf[0]], writes=[b_bo])
        for w_d, wTO, bwT, ntile, tw, ost in (
            (wg_d, WGO, b_wgqT, 22, D, I), (wu_d, WUO, b_wuqT, 22, D, I),
            (wd_d, WDO, b_wdqT, 8, I, D),
        ):
            for it in range(ntile):
                wst, bst = x_t[it % 2], b_x[it % 2]
                hqt, bhqt = hq[it % 2], b_hq[it % 2]
                for poff in range(0, tw, D):
                    cw = min(D, tw - poff)
                    kb.dma(sy, wst[:, 0:cw], w_d[it * 128:(it + 1) * 128, poff:poff + cw],
                           writes=[bst])
                    emit_quant(kb, q_t[it % 2], B, wst[:, 0:cw], hqt[:, poff:poff + cw],
                               bst, bhqt, cw, par=it % 2)
                nblk = tw // 128
                dst = bass.AP(ffnw, wTO + it * 128,
                              [[FST, 128], [ost, nblk], [1, 128]])
                src = hqt[:, 0:tw].rearrange("p (c q) -> p c q", c=nblk)
                kb.dma(act, dst, src, transpose=True, reads=[bhqt], writes=[bwT])
        pg = [es1.enter_context(nc.psum_tensor(f"pg{p}", [128, 512], F32)) for p in range(2)]
        pu = [es1.enter_context(nc.psum_tensor(f"pu{p}", [128, 512], F32)) for p in range(2)]
        pd = [es1.enter_context(nc.psum_tensor(f"pd{p}", [128, D], F32)) for p in range(2)]
        b_pg = [buf(f"pg{p}") for p in range(2)]
        b_pu = [buf(f"pu{p}") for p in range(2)]
        b_pd = [buf(f"pd{p}") for p in range(2)]

        for it in range(nt):
            for sq in range(n_seq):
                tp = sq % 2
                row = slice(it * 128, (it + 1) * 128)
                xt, bx = x_t[tp], b_x[tp]
                xnfp, bxnf = xnf[tp], b_xnf[tp]
                xnbt, bxnbt = xnbp[tp], b_xnbp[tp]
                xqt, bxq = xq[tp], b_xq[tp]
                xqf, bxqf = xq_fm[tp], b_xqfm[tp]
                hqt, bhq = hq[tp], b_hq[tp]
                hqf, bhqf = hq_fm[tp], b_hqfm[tp]
                hfl_ap = hqf[:].bitcast(F16)
                ptt, bpt = part_p[tp], b_partp[tp]
                kb.dma(sy, xt[:], x_d[sq, row, :], writes=[bx])
                emit_ln(kb, small[tp], ln_b[tp], xt[:], xnbt[:], xnfp[:], bx, bxnbt, bxnf)
                kb.dma(pool, xnb_d[sq, row, :], xnfp[:], reads=[bxnf], writes=[buf("xnb_d")])
                emit_quant(kb, q_t[tp], B, xnfp[:], xqt[:], bxnf, bxq, D, par=tp)
                kb.dma(act, xqf[:].rearrange("p (c q) -> p c q", c=8),
                       xqt[:].rearrange("p (c q) -> p c q", c=8),
                       transpose=True, reads=[bxq], writes=[bxqf])
                for ci, (off, cw) in enumerate(ICHUNKS):
                    cp = ci % 2
                    for dc in range(8):
                        lhsT = xqf[:, dc * 128:(dc + 1) * 128]
                        kb.op(pe, "matmul", pg[cp][:, 0:cw], lhsT,
                              wgqT[:, dc * I + off: dc * I + off + cw],
                              start=(dc == 0), stop=(dc == 7),
                              reads=[bxqf, b_wgqT], writes=[b_pg[cp]])
                        kb.op(pe, "matmul", pu[cp][:, 0:cw], lhsT,
                              wuqT[:, dc * I + off: dc * I + off + cw],
                              start=(dc == 0), stop=(dc == 7),
                              reads=[bxqf, b_wuqT], writes=[b_pu[cp]])
                    kb.op(act, "activation", h_sg[tp][:, 0:cw], pg[cp][:, 0:cw], AF.Tanh,
                          scale=0.5, reads=[b_pg[cp]], writes=[b_hsg[tp]])
                    kb.op(act, "copy", h_gt[tp][:, 0:cw], pg[cp][:, 0:cw],
                          reads=[b_pg[cp]], writes=[b_hgt[tp]])
                    kb.op(act, "copy", h_up[tp][:, 0:cw], pu[cp][:, 0:cw],
                          reads=[b_pu[cp]], writes=[b_hup[tp]])
                    kb.op(dve, "tensor_scalar", h_sg[tp][:, 0:cw], h_sg[tp][:, 0:cw], 0.5, 0.5,
                          ALU.mult, ALU.add, reads=[b_hsg[tp]], writes=[b_hsg[tp]])
                    kb.op(dve, "tensor_mul", h_gt[tp][:, 0:cw], h_gt[tp][:, 0:cw], h_up[tp][:, 0:cw],
                          reads=[b_hgt[tp], b_hup[tp]], writes=[b_hgt[tp]])
                    kb.op(dve, "tensor_mul", hfl_ap[:, off:off + cw], h_gt[tp][:, 0:cw],
                          h_sg[tp][:, 0:cw], reads=[b_hgt[tp], b_hsg[tp]], writes=[bhqf])
                for qoff in range(0, I, QW):
                    cw = min(QW, I - qoff)
                    emit_quant(kb, q_t[tp], B, hfl_ap[:, qoff:qoff + cw],
                               hqt[:, qoff:qoff + cw], bhqf, bhq, cw, par=tp)
                kb.dma(act, hqf[:].rearrange("p (c q) -> p c q", c=22),
                       hqt[:].rearrange("p (c q) -> p c q", c=22),
                       transpose=True, reads=[bhq], writes=[bhqf])
                for half in range(2):
                    for ic in range(22):
                        kb.op(pe, "matmul", pd[tp][:, half * 512:(half + 1) * 512],
                              hqf[:, ic * 128:(ic + 1) * 128],
                              wdqT[:, ic * D + half * 512: ic * D + (half + 1) * 512],
                              start=(ic == 0), stop=(ic == 21),
                              reads=[bhqf, b_wdqT], writes=[b_pd[tp]])
                kb.op(dve, "tensor_add", ptt[:], pd[tp][:], xt[:],
                      reads=[b_pd[tp], bx], writes=[bpt])
                kb.op(dve, "tensor_add", ptt[:], ptt[:], bias_out[:],
                      reads=[bpt, b_bo], writes=[bpt])
                dst = (out_d if only_pass1 else part_d)[sq, row, :]
                kb.dma(sy, dst, ptt[:], reads=[bpt],
                       writes=([] if only_pass1 else [buf("part_d")]))

    if only_pass1:
        kb.finalize()
        es.close()
        return nc

    env = dict(locals()); env['sbt'] = sbt; env['dbg'] = dbg
    env['no_dyadic'] = no_dyadic; env['no_c0'] = no_c0
    build_pass2(nc, es, kb, B, buf, env)
    kb.finalize()
    es.close()
    return nc


def build_pass2(nc, es, kb, B, buf, env):
    sbt = env["sbt"]
    small = env["small"]
    xnb, part_t = env["xnb"], env["part_t"]
    x_d, out_d, part_d, xnb_d = env["x_d"], env["out_d"], env["part_d"], env["xnb_d"]
    wqkv_d, bqkv_d = env["wqkv_d"], env["bqkv_d"]
    wgt_d, bgt_d, wo_d = env["wgt_d"], env["bgt_d"], env["wo_d"]
    bwin_d, bdyn_d = env["bwin_d"], env["bdyn_d"]
    n_seq, nt = env["n_seq"], env["nt"]
    no_dyadic = env.get("no_dyadic"); no_c0 = env.get("no_c0")
    pe, dve, act, pool, sy = kb.pe, kb.dve, kb.act, kb.pool, kb.sy
    b_partd, b_xnbd = buf("part_d"), buf("xnb_d")
    NEG = -1e30

    with ExitStack() as es2:
        s2t = lambda name, shape, dt=BF16: es2.enter_context(nc.sbuf_tensor(name, shape, dt))
        attnw = s2t("attnw", [128, 8 * 3 * D + 16 * D])
        ATST = 8 * 3 * D + 16 * D
        WQO, WGO2, WOO = 0, 8 * 3 * D, 8 * 3 * D + 8 * D
        wqkvT = attnw[:, WQO:WQO + 24 * D]
        wgT = attnw[:, WGO2:WGO2 + 8 * D]
        woT = attnw[:, WOO:WOO + 8 * D]
        b_wq, b_wg2, b_wo = buf("wqkvT"), buf("wgT"), buf("woT")
        k_tm = s2t("k_tm", [128, 16 * D])
        v_tm = s2t("v_tm", [128, 16 * D])
        kfm = s2t("kfm", [128, 3 * D])
        ebw = s2t("ebw", [128, H * WCH * 128])
        ebd = s2t("ebd", [128, H * ND])
        bias_qkv = s2t("bias_qkv", [128, 3 * D])
        bias_gate = s2t("bias_gate", [128, D])
        ones_col = s2t("ones_col", [128, 1])
        qtm1 = s2t("q_tm", [128, D])
        q_tm = [qtm1, qtm1]
        q_fm = [s2t(f"q_fm{p}", [128, D]) for p in range(2)]
        e4 = [s2t(f"e4{p}", [128, 4 * WCH * 128]) for p in range(2)]
        sdyn = s2t("sdyn", [128, H * ND], F32)
        edyn = s2t("edyn", [128, H * ND])
        dsum = s2t("dsum", [128, H], F32)
        ssum = s2t("ssum", [128, H], F32)
        srec = [s2t(f"srec{p}", [128, H], F32) for p in range(2)]
        g1 = s2t("g1", [128, D])
        g21 = s2t("g2", [128, D])
        g2 = [g21, g21]
        g41 = s2t("g4", [128, D])
        g4 = [g41, g41]
        flat_sb = [s2t(f"flat_sb{p}", [128, D]) for p in range(2)]
        fg1t = s2t("fg", [128, D])
        fg = [fg1t, fg1t]
        fgf1 = s2t("fg_fm", [128, D])
        fg_fm = [fgf1, fgf1]
        xnb2p = [xnb, xnb]
        xnbfm2 = [s2t(f"xnbfm2{p}", [128, D]) for p in range(2)]
        part2p = [part_t, part_t]
        b_part2p = [B["part_t"], B["part_t"]]
        wstage2 = part_t
        b_ws2 = B["part_t"]
        b_ktm = [buf(f"k_tm{j}") for j in range(16)]
        b_vtm = [buf(f"v_tm{j}") for j in range(16)]
        b_kfm = [buf(f"kfm{j}") for j in range(3)]
        b_ebw, b_ebd = buf("ebw"), buf("ebd")
        b_bq, b_bg = buf("bias_qkv"), buf("bias_gate")
        b_ones = buf("ones_col")
        b_qtm1 = buf("q_tm")
        b_qtm = [b_qtm1, b_qtm1]
        b_qfm = [buf(f"q_fm{p}") for p in range(2)]
        b_e4 = [buf(f"e4{p}") for p in range(2)]
        b_sdyn, b_edyn = buf("sdyn"), buf("edyn")
        b_dsum, b_ssum = buf("dsum"), buf("ssum")
        b_srec = [buf(f"srec{p}") for p in range(2)]
        b_g1 = buf("g1")
        b_g21 = buf("g2")
        b_g2 = [b_g21, b_g21]
        b_g41 = buf("g4")
        b_g4 = [b_g41, b_g41]
        b_flat = [buf(f"flat_sb{p}") for p in range(2)]
        b_fg1 = buf("fg")
        b_fg = [b_fg1, b_fg1]
        b_fgf1 = buf("fg_fm")
        b_fgfm = [b_fgf1, b_fgf1]
        b_xnb2p = [B["xnb"], B["xnb"]]
        b_xnbfm2 = [buf(f"xnbfm2{p}") for p in range(2)]

        P1 = es2.enter_context(nc.psum_tensor("P1", [128, 2048], F32))
        P2 = es2.enter_context(nc.psum_tensor("P2", [128, D], F32))
        P3 = es2.enter_context(nc.psum_tensor("P3", [128, 512], F32))
        P4 = es2.enter_context(nc.psum_tensor("P4", [128, 512], F32))
        b_P1b = [buf(f"P1b{b}") for b in range(4)]
        b_P2, b_P3, b_P4 = buf("P2"), buf("P3"), buf("P4")
        qkv_slots = [(P1[:, 0:512], b_P1b[0]), (P1[:, 512:1024], b_P1b[1]),
                     (P4[:, 0:512], b_P4)]

        kb.op(dve, "memset", ones_col[:], 1.0, writes=[b_ones])
        kb.op(dve, "memset", P1[:], 0.0, writes=b_P1b)
        kb.op(dve, "memset", P2[:], 0.0, writes=[b_P2])
        kb.op(dve, "memset", P3[:], 0.0, writes=[b_P3])
        kb.op(dve, "memset", P4[:], 0.0, writes=[b_P4])
        wcv = [(xnb, B["xnb"]), (q_tm[0], b_qtm[0])]
        for it in range(24):
            cv, bcv = wcv[it % 2]
            if it < 8:
                kb.dma(sy, wstage2[:], wqkv_d[it * 128:(it + 1) * 128, :], writes=[b_ws2])
                kb.op(act, "activation", cv[:], wstage2[:], AF.Identity, scale=0.125,
                      reads=[b_ws2], writes=[bcv])
            else:
                kb.dma(pool, cv[:], wqkv_d[it * 128:(it + 1) * 128, :], writes=[bcv])
            dst = bass.AP(attnw, WQO + it * 128, [[ATST, 128], [3 * D, 8], [1, 128]])
            kb.dma(act, dst, cv[:].rearrange("p (c q) -> p c q", c=8),
                   transpose=True, reads=[bcv], writes=[b_wq])
        for wi, (w_d, base, bw) in enumerate(((wgt_d, WGO2, b_wg2), (wo_d, WOO, b_wo))):
            for it in range(8):
                cv, bcv = wcv[it % 2]
                kb.dma(pool, cv[:], w_d[it * 128:(it + 1) * 128, :], writes=[bcv])
                dst = bass.AP(attnw, base + it * 128, [[ATST, 128], [D, 8], [1, 128]])
                kb.dma(act, dst, cv[:].rearrange("p (c q) -> p c q", c=8),
                       transpose=True, reads=[bcv], writes=[bw])
        for c in range(3):
            kb.dma(sy, wstage2[:, 0:D], bqkv_d[:, c * D:(c + 1) * D], writes=[b_ws2])
            kb.op(act, "activation", bias_qkv[:, c * D:(c + 1) * D], wstage2[:, 0:D],
                  AF.Identity, scale=(0.125 if c == 0 else 1.0),
                  reads=[b_ws2], writes=[b_bq])
        kb.dma(sy, wstage2[:, 0:D], bgt_d, writes=[b_ws2])
        kb.op(act, "copy", bias_gate[:], wstage2[:, 0:D], reads=[b_ws2], writes=[b_bg])
        for ch in range(6):
            src = bass.AP(bwin_d.tensor, ch * 8 * 128 * 128,
                          [[128, 128], [128 * 128, 8], [1, 128]])
            kb.dma(sy, wstage2[:, 0:1024], src, writes=[b_ws2])
            kb.op(act, "activation", ebw[:, ch * 1024:(ch + 1) * 1024],
                  wstage2[:, 0:1024], AF.Exp, reads=[b_ws2], writes=[b_ebw])
        kb.dma(sy, wstage2[:, 0:H * ND], bdyn_d, writes=[b_ws2])
        kb.op(act, "activation", ebd[:], wstage2[:, 0:H * ND], AF.Exp,
              reads=[b_ws2], writes=[b_ebd])

        for s in range(n_seq):
            for it in range(nt):
                tp = it % 2
                row = slice(it * 128, (it + 1) * 128)
                ptt, bpt = part2p[tp], b_part2p[tp]
                xnbt, bxnbt = xnb2p[tp], b_xnb2p[tp]
                xfm, bxfm = xnbfm2[tp], b_xnbfm2[tp]
                qtm, bqtm = q_tm[tp], b_qtm[tp]
                qfm, bqfm = q_fm[tp], b_qfm[tp]
                kb.dma(sy, ptt[:], part_d[s, row, :], reads=[b_partd], writes=[bpt])
                kb.dma(sy, xnbt[:], xnb_d[s, row, :], reads=[b_xnbd], writes=[bxnbt])
                kb.dma(act, xfm[:].rearrange("p (c q) -> p c q", c=8),
                       xnbt[:].rearrange("p (c q) -> p c q", c=8),
                       transpose=True, reads=[bxnbt], writes=[bxfm])
                for g in range(6):
                    ps_g, b_ps = qkv_slots[g % 3]
                    for dc in range(8):
                        kb.op(pe, "matmul", ps_g, xfm[:, dc * 128:(dc + 1) * 128],
                              wqkvT[:, dc * 3 * D + g * 512: dc * 3 * D + (g + 1) * 512],
                              start=(dc == 0), stop=(dc == 7),
                              reads=[bxfm, b_wq], writes=[b_ps])
                    if g < 2:
                        dstap, dbuf = qtm[:, g * 512:(g + 1) * 512], bqtm
                    elif g < 4:
                        dstap = k_tm[:, it * D + (g - 2) * 512: it * D + (g - 1) * 512]
                        dbuf = b_ktm[it]
                    else:
                        dstap = v_tm[:, it * D + (g - 4) * 512: it * D + (g - 3) * 512]
                        dbuf = b_vtm[it]
                    kb.op(dve, "scalar_tensor_tensor", dstap, ps_g, 1.0,
                          bias_qkv[:, g * 512:(g + 1) * 512], ALU.mult, ALU.add,
                          reads=[b_ps, b_bq], writes=[dbuf])
                kb.dma(act, qfm[:].rearrange("p (c q) -> p c q", c=8),
                       qtm[:].rearrange("p (c q) -> p c q", c=8),
                       transpose=True, reads=[bqtm], writes=[bqfm])
                kb.dma(act, kfm[:, (it % 3) * D:(it % 3 + 1) * D].rearrange("p (c q) -> p c q", c=8),
                       k_tm[:, it * D:(it + 1) * D].rearrange("p (c q) -> p c q", c=8),
                       transpose=True, reads=[b_ktm[it]], writes=[b_kfm[it % 3]])
                kb.op(dve, "memset", sdyn[:], NEG, writes=[b_sdyn])
                for mi, o in enumerate(DYADIC):
                    ot = o // 128
                    if it < ot or no_dyadic:
                        continue
                    kb.op(dve, "tensor_mul", g1[:], qtm[:],
                          k_tm[:, (it - ot) * D:(it - ot + 1) * D],
                          reads=[bqtm, b_ktm[it - ot]], writes=[b_g1])
                    kb.op(dve, "reduce_sum", sdyn[:, mi * H:(mi + 1) * H],
                          g1[:].rearrange("p (h d) -> p h d", d=HD), AX.X,
                          reads=[b_g1], writes=[b_sdyn])
                kb.op(act, "activation", edyn[:], sdyn[:], AF.Exp,
                      reads=[b_sdyn], writes=[b_edyn])
                kb.op(dve, "tensor_mul", edyn[:], edyn[:], ebd[:],
                      reads=[b_edyn, b_ebd], writes=[b_edyn])
                ev = bass.AP(edyn, 0, [[edyn[:].ap[0][0], 128], [1, H], [H, ND]])
                kb.op(dve, "reduce_sum", dsum[:], ev, AX.X,
                      reads=[b_edyn], writes=[b_dsum])
                for hg in range(4):
                    ep = hg % 2
                    heads = range(4 * hg, 4 * hg + 4)
                    for hh, h in sorted(enumerate(heads), key=lambda p: p[1] % 2):
                        for c in range(WCH):
                            jt = it - 2 + c
                            reg = P1[:, hh * 512 + c * 128: hh * 512 + (c + 1) * 128]
                            if jt < 0 or (no_c0 and c == 0):
                                kb.op(dve, "memset", reg, NEG, writes=[b_P1b[hh]])
                                continue
                            pb = (h % 2) * 64
                            lhsT = kfm[pb:pb + 64,
                                       (jt % 3) * D + (h // 2) * 128:(jt % 3) * D + (h // 2) * 128 + 128]
                            rhs = qfm[pb:pb + 64, (h // 2) * 128:(h // 2) * 128 + 128]
                            kb.op(pe, "matmul", reg, lhsT, rhs, start=True, stop=True,
                                  reads=[b_kfm[jt % 3], bqfm], writes=[b_P1b[hh]])
                    pv = P1[:].rearrange("p (a r) -> p a r", r=512)[:, :, 0:WCH * 128]
                    ev4 = e4[ep][:].rearrange("p (a r) -> p a r", r=WCH * 128)
                    kb.op(act, "activation", ev4, pv, AF.Exp,
                          reads=b_P1b, writes=[b_e4[ep]])
                    ebv = ebw[:, (4 * hg) * WCH * 128:(4 * hg + 4) * WCH * 128]
                    kb.op(dve, "tensor_mul", e4[ep][:], e4[ep][:], ebv,
                          reads=[b_e4[ep], b_ebw], writes=[b_e4[ep]])
                    for hh, h in enumerate(heads):
                        valid = [c for c in range(WCH)
                                 if it - 2 + c >= 0 and not (no_c0 and c == 0)]
                        for vi, c in enumerate(valid):
                            jt = it - 2 + c
                            ech = e4[ep][:, hh * 384 + c * 128: hh * 384 + (c + 1) * 128]
                            kb.op(pe, "matmul", P3[:, h:h + 1], ech, ones_col[:],
                                  start=(vi == 0), stop=(vi == len(valid) - 1),
                                  skip_group_check=True,
                                  reads=[b_e4[ep], b_ones], writes=[b_P3])
                            kb.op(pe, "matmul", P2[:, h * HD:(h + 1) * HD], ech,
                                  v_tm[:, jt * D + h * HD: jt * D + (h + 1) * HD],
                                  start=(vi == 0), stop=(vi == len(valid) - 1),
                                  skip_group_check=True,
                                  reads=[b_e4[ep], b_vtm[jt]], writes=[b_P2])
                kb.op(dve, "tensor_add", ssum[:], P3[:, 0:H], dsum[:],
                      reads=[b_P3, b_dsum], writes=[b_ssum])
                kb.op(dve, "reciprocal", srec[tp][:], ssum[:],
                      reads=[b_ssum], writes=[b_srec[tp]])
                vd = [] if no_dyadic else [o for o in DYADIC if it >= o // 128]
                if not vd:
                    kb.op(dve, "memset", g2[tp][:], 0.0, writes=[b_g2[tp]])
                for vi, o in enumerate(vd):
                    mi = DYADIC.index(o)
                    e_b = edyn[:, mi * H:(mi + 1) * H]
                    e_bc = e_b.unsqueeze(2).broadcast_to([128, H, HD])
                    tgt, tb = (g2[tp], b_g2[tp]) if vi == 0 else (g1, b_g1)
                    kb.op(dve, "tensor_tensor",
                          tgt[:].rearrange("p (h d) -> p h d", d=HD),
                          v_tm[:, (it - o // 128) * D:(it - o // 128 + 1) * D]
                          .rearrange("p (h d) -> p h d", d=HD),
                          e_bc, ALU.mult,
                          reads=[b_vtm[it - o // 128], b_edyn], writes=[tb])
                    if vi > 0:
                        kb.op(dve, "tensor_add", g2[tp][:], g2[tp][:], g1[:],
                              reads=[b_g2[tp], b_g1], writes=[b_g2[tp]])
                srec_bc = srec[tp][:].unsqueeze(2).broadcast_to([128, H, HD])
                v3h = lambda ap: ap.rearrange("p (h d) -> p h d", d=HD)
                kb.op(dve, "tensor_tensor", v3h(flat_sb[tp][:]), v3h(P2[:]), srec_bc,
                      ALU.mult, reads=[b_P2, b_srec[tp]], writes=[b_flat[tp]])
                kb.op(dve, "tensor_tensor", v3h(g2[tp][:]), v3h(g2[tp][:]), srec_bc,
                      ALU.mult, reads=[b_g2[tp], b_srec[tp]], writes=[b_g2[tp]])
                kb.op(dve, "tensor_add", flat_sb[tp][:], flat_sb[tp][:], g2[tp][:],
                      reads=[b_flat[tp], b_g2[tp]], writes=[b_flat[tp]])
                for g in range(2):
                    for dc in range(8):
                        kb.op(pe, "matmul", P2[:, g * 512:(g + 1) * 512],
                              xfm[:, dc * 128:(dc + 1) * 128],
                              wgT[:, dc * D + g * 512: dc * D + (g + 1) * 512],
                              start=(dc == 0), stop=(dc == 7),
                              reads=[bxfm, b_wg2], writes=[b_P2])
                kb.op(dve, "tensor_add", g4[tp][:], P2[:, 0:D], bias_gate[:],
                      reads=[b_P2, b_bg], writes=[b_g4[tp]])
                kb.op(act, "activation", g4[tp][:], g4[tp][:], AF.Tanh, scale=0.5,
                      reads=[b_g4[tp]], writes=[b_g4[tp]])
                kb.op(dve, "tensor_scalar", g4[tp][:], g4[tp][:], 0.5, 0.5,
                      ALU.mult, ALU.add, reads=[b_g4[tp]], writes=[b_g4[tp]])
                kb.op(dve, "tensor_mul", fg[tp][:], flat_sb[tp][:], g4[tp][:],
                      reads=[b_flat[tp], b_g4[tp]], writes=[b_fg[tp]])
                kb.dma(act, fg_fm[tp][:].rearrange("p (c q) -> p c q", c=8),
                       fg[tp][:].rearrange("p (c q) -> p c q", c=8),
                       transpose=True, reads=[b_fg[tp]], writes=[b_fgfm[tp]])
                for g in range(2):
                    for dc in range(8):
                        kb.op(pe, "matmul", P2[:, g * 512:(g + 1) * 512],
                              fg_fm[tp][:, dc * 128:(dc + 1) * 128],
                              woT[:, dc * D + g * 512: dc * D + (g + 1) * 512],
                              start=(dc == 0), stop=(dc == 7),
                              reads=[b_fgfm[tp], b_wo], writes=[b_P2])
                kb.op(dve, "tensor_add", ptt[:], P2[:], ptt[:],
                      reads=[b_P2, bpt], writes=[bpt])
                kb.dma(sy, out_d[s, row, :], ptt[:], reads=[bpt])


_NC_CACHE = {}


def kernel(**inputs):
    maps = core_inputs(inputs, n_cores=8)
    if "nc" not in _NC_CACHE:
        _NC_CACHE["nc"] = build(n_seq=2, nt=16, only_pass1=False)
    nc = _NC_CACHE["nc"]
    res = run_bass_kernel_spmd(nc, maps, core_ids=list(range(8)))
    out = np.concatenate([np.asarray(r["out"]) for r in res.results], axis=0)
    return out.astype(np.float32)


# version 06
# speedup vs baseline: 1.3480x; 1.2017x over previous
import numpy as np
from contextlib import ExitStack

import concourse.bass as bass
import concourse.mybir as mybir
from concourse.bass_utils import run_bass_kernel_spmd

"""Host-side input prep shared by dev scripts and kernel.py."""

_DENSE_LOCAL_W = 64
_DYADIC_ALL = [96, 128, 192, 256, 384, 512, 768, 1024, 1536]
OFFSETS = sorted(set(range(0, _DENSE_LOCAL_W + 1)) | set(_DYADIC_ALL))
WIN_OFF = set(range(0, 65)) | {96, 128, 192}
DYADIC = [256, 384, 512, 768, 1024, 1536]
H = 16
NEG = -1e30


def make_bwin(pos_bias):
    pb = {o: pos_bias[m] for m, o in enumerate(OFFSETS)}
    out = np.full((H, 3, 128, 128), NEG, dtype=np.float32)
    for c in range(3):
        base = (2 - c) * 128
        for j in range(128):
            for i in range(128):
                off = base + i - j
                if off in WIN_OFF:
                    out[:, c, j, i] = pb[off]
    return out


def make_bdyn(pos_bias):
    idx = {o: m for m, o in enumerate(OFFSETS)}
    v = np.empty((len(DYADIC), H), dtype=np.float32)
    for mi, o in enumerate(DYADIC):
        v[mi, :] = pos_bias[idx[o]]
    row = v.reshape(-1)
    return np.broadcast_to(row, (128, H * len(DYADIC))).copy()


def core_inputs(ins, n_cores=8):
    x = np.ascontiguousarray(ins["L0_attn_x"], dtype=np.float32)
    bwin = np.ascontiguousarray(make_bwin(np.asarray(ins["L0_attn_pos_bias"])))
    bdyn = np.ascontiguousarray(make_bdyn(np.asarray(ins["L0_attn_pos_bias"])))
    com = {
        "wg": np.ascontiguousarray(ins["L0_ffn_w_gate"], dtype=np.float32),
        "wu": np.ascontiguousarray(ins["L0_ffn_w_up"], dtype=np.float32),
        "wd": np.ascontiguousarray(ins["L0_ffn_w_down"], dtype=np.float32),
        "wqkv": np.ascontiguousarray(ins["L0_attn_Wqkv"], dtype=np.float32),
        "bqkvb": np.ascontiguousarray(
            np.broadcast_to(ins["L0_attn_bqkv"], (128, 3072))).astype(np.float32),
        "wgate": np.ascontiguousarray(ins["L0_attn_Wgate"], dtype=np.float32),
        "bgateb": np.ascontiguousarray(
            np.broadcast_to(ins["L0_attn_bgate"], (128, 1024))).astype(np.float32),
        "wout": np.ascontiguousarray(ins["L0_attn_Wout"], dtype=np.float32),
        "boutb": np.ascontiguousarray(
            np.broadcast_to(ins["L0_attn_bout"], (128, 1024))).astype(np.float32),
        "bwin": bwin,
        "bdyn": bdyn,
    }
    per = x.shape[0] // n_cores
    maps = []
    for c in range(n_cores):
        m = dict(com)
        m["x"] = np.ascontiguousarray(x[c * per:(c + 1) * per])
        maps.append(m)
    return maps


"""Kernel-builder helpers: per-engine semaphore bookkeeping + dependency tracking.

Compute instructions increment their engine's progress semaphore (+1 each).
DMA queues use R rotating semaphores with issue-gating: a DMA on rotation
slot j may only issue once the previous DMA on slot j completed, so each
slot's semaphore is serialized and every 16*m boundary is a deterministic
wait value (concurrent DMAs on ONE semaphore post their 16 increments
interleaved, making intermediate waits racy).
"""

NDMASEM = 6


class Buf:
    __slots__ = ("name", "writer", "readers")

    def __init__(self, name):
        self.name = name
        self.writer = None
        self.readers = []


class Eng:
    def __init__(self, kb, name, eng, sem, dma_sems):
        self.kb = kb
        self.name = name
        self.e = eng
        self.sem = sem
        self.dma_sems = dma_sems
        self.dma_counts = [0] * len(dma_sems)
        self.dma_idx = 0
        self.n = 0
        self.known = {}

    def _wait(self, token):
        sem, val = token
        if self.name == "pe" and sem is self.sem:
            return
        if self.known.get(id(sem), -1) >= val:
            return
        self.e.wait_ge(sem, val)
        self.known[id(sem)] = val


class KB:
    def __init__(self, nc, es):
        self.nc = nc
        mk = lambda nm: es.enter_context(nc.semaphore(nm))
        self.pe = Eng(self, "pe", nc.tensor, mk("sem_pe"), [])
        self.dve = Eng(self, "dve", nc.vector, mk("sem_dve"), [])
        self.act = Eng(self, "act", nc.scalar, mk("sem_act"),
                       [mk(f"dsc{i}") for i in range(NDMASEM)])
        self.pool = Eng(self, "pool", nc.gpsimd, mk("sem_pool"),
                        [mk(f"dg{i}") for i in range(NDMASEM)])
        self.sy = Eng(self, "sy", nc.sync, mk("sem_sy"),
                      [mk(f"dsy{i}") for i in range(NDMASEM)])
        self.engines = [self.pe, self.dve, self.act, self.pool, self.sy]

    def op(self, eng, method, *args, reads=(), writes=(), **kwargs):
        for b in reads:
            if b.writer is not None:
                eng._wait(b.writer)
        for b in writes:
            if b.writer is not None:
                eng._wait(b.writer)
            for t in b.readers:
                eng._wait(t)
        inst = getattr(eng.e, method)(*args, **kwargs)
        eng.n += 1
        inst.then_inc(eng.sem, 1)
        tok = (eng.sem, eng.n)
        for b in reads:
            b.readers.append(tok)
        for b in writes:
            b.writer = tok
            b.readers = []
        return tok

    def dma(self, eng, dst, src, reads=(), writes=(), transpose=False, **kw):
        for b in reads:
            if b.writer is not None:
                eng._wait(b.writer)
        for b in writes:
            if b.writer is not None:
                eng._wait(b.writer)
            for t in b.readers:
                eng._wait(t)
        j = eng.dma_idx % len(eng.dma_sems)
        eng.dma_idx += 1
        sem = eng.dma_sems[j]
        if eng.dma_counts[j] > 0:
            eng._wait((sem, 16 * eng.dma_counts[j]))
        if transpose:
            inst = eng.e.dma_start_transpose(dst, src)
        else:
            inst = eng.e.dma_start(dst, src, **kw)
        eng.dma_counts[j] += 1
        inst.then_inc(sem, 16)
        tok = (sem, 16 * eng.dma_counts[j])
        for b in reads:
            b.readers.append(tok)
        for b in writes:
            b.writer = tok
            b.readers = []
        return tok

    def barrier_token(self, eng):
        return (eng.sem, eng.n)

    def finalize(self):
        for e in self.engines:
            for j, sem in enumerate(e.dma_sems):
                if e.dma_counts[j] > 0:
                    e._wait((sem, 16 * e.dma_counts[j]))


def full_barrier(kb):
    toks = []
    for e in kb.engines:
        if e.n > 0:
            toks.append((e.sem, e.n))
        for j, sem in enumerate(e.dma_sems):
            if e.dma_counts[j] > 0:
                toks.append((sem, 16 * e.dma_counts[j]))
    for e in kb.engines:
        for t in toks:
            e._wait(t)


"""Megakernel builder for mk_prb_d1: out = x + attn(LN(x)) + ffn(LN(x)).

Per core: n_seq sequences of 2048 tokens. Two weight-phased passes inside ONE
program: pass1 = FFN (fp4 SwiGLU) writing partial = x + ffn + bout to DRAM
scratch (plus xn bf16 cache); pass2 = sparse attention reading partial/xn and
writing out.
"""

F32 = mybir.dt.float32
BF16 = mybir.dt.bfloat16
F16 = mybir.dt.float16
ALU = mybir.AluOpType
AX = mybir.AxisListType
AF = mybir.ActivationFunctionType

D = 1024
N = 2048
I = 2816
ICHUNKS = [(0, 512), (512, 512), (1024, 512), (1536, 512), (2048, 512), (2560, 256)]
H = 16
HD = 64
WCH = 3
DYADIC = [256, 384, 512, 768, 1024, 1536]
ND = len(DYADIC)
EPS = 1e-5
QW = 1408


def emit_ln(kb, t, b, x_ap, xnb_ap, xnf_ap, bx, bxnb, bxnf):
    dve = kb.dve
    kb.op(dve, "reduce_sum", t["s1"][:], x_ap, AX.X, reads=[bx], writes=[b["s1"]])
    kb.op(dve, "tensor_mul", t["xsq"][:], x_ap, x_ap, reads=[bx], writes=[b["xsq"]])
    kb.op(dve, "reduce_sum", t["s2"][:], t["xsq"][:], AX.X, reads=[b["xsq"]], writes=[b["s2"]])
    kb.op(dve, "tensor_scalar", t["mean"][:], t["s1"][:], 1.0 / D, None, ALU.mult,
          reads=[b["s1"]], writes=[b["mean"]])
    kb.op(dve, "tensor_mul", t["tmp1"][:], t["mean"][:], t["mean"][:],
          reads=[b["mean"]], writes=[b["tmp1"]])
    kb.op(dve, "scalar_tensor_tensor", t["var"][:], t["s2"][:], 1.0 / D, t["tmp1"][:],
          ALU.mult, ALU.subtract, reads=[b["s2"], b["tmp1"]], writes=[b["var"]])
    kb.op(dve, "tensor_scalar", t["var"][:], t["var"][:], 1.0, EPS, ALU.mult, ALU.add,
          reads=[b["var"]], writes=[b["var"]])
    kb.op(dve, "tensor_scalar", t["y"][:], t["var"][:], -0.5, 1.5, ALU.mult, ALU.add,
          reads=[b["var"]], writes=[b["y"]])
    for _ in range(2):
        kb.op(dve, "tensor_mul", t["tmp1"][:], t["y"][:], t["y"][:],
              reads=[b["y"]], writes=[b["tmp1"]])
        kb.op(dve, "tensor_mul", t["tmp1"][:], t["tmp1"][:], t["var"][:],
              reads=[b["tmp1"], b["var"]], writes=[b["tmp1"]])
        kb.op(dve, "tensor_scalar", t["tmp1"][:], t["tmp1"][:], -0.5, 1.5, ALU.mult, ALU.add,
              reads=[b["tmp1"]], writes=[b["tmp1"]])
        kb.op(dve, "tensor_mul", t["y"][:], t["y"][:], t["tmp1"][:],
              reads=[b["y"], b["tmp1"]], writes=[b["y"]])
    kb.op(dve, "scalar_tensor_tensor", t["nmr"][:], t["mean"][:], -1.0, t["y"][:],
          ALU.mult, ALU.mult, reads=[b["mean"], b["y"]], writes=[b["nmr"]])
    kb.op(dve, "tensor_scalar", xnf_ap, x_ap, t["y"][:], t["nmr"][:], ALU.mult, ALU.add,
          reads=[bx, b["y"], b["nmr"]], writes=[bxnf])


def emit_quant(kb, t, b, xs_ap, xq_ap, bxs, bxq, width, par=0):
    dve, act = kb.dve, kb.act
    nb = width // 16
    W = width
    I16 = mybir.dt.int16
    v3 = lambda ap: ap.rearrange("p (a c) -> p a c", c=16)
    xa, qa, qb_, qd = (t["qxa"][:, :W], t["qa"][:, :W], t["qb"][:, :W], t["qd"][:, :W])
    m, mc, r6f = t["qm"][:, :nb], t["qmc"][:, :nb], t["qr6f"][:, :nb]
    r6h, sbh = t["qr6h"][:, :nb], t["qsbh"][:, :nb]
    B = {k: b[f"q_{k}"] for k in ("m", "mc", "r6f", "r6h", "sbh", "a", "b", "d")}
    B["xa"] = b[f"q_xa{par}"]
    B["mc"] = B["m"]
    kb.op(act, "activation", xa, xs_ap, AF.Abs, reads=[bxs], writes=[B["xa"]])
    kb.op(dve, "reduce_max", m, v3(xa), AX.X, reads=[B["xa"]], writes=[B["m"]])
    kb.op(dve, "tensor_scalar", mc, m, 1e-4, None, ALU.max, reads=[B["m"]], writes=[B["mc"]])
    kb.op(dve, "tensor_scalar", sbh, mc, 1.0 / 6.0, None, ALU.mult,
          reads=[B["mc"]], writes=[B["sbh"]])
    kb.op(dve, "reciprocal", r6f, mc, reads=[B["mc"]], writes=[B["r6f"]])
    kb.op(dve, "tensor_scalar", r6h, r6f, 6.0, None, ALU.mult,
          reads=[B["r6f"]], writes=[B["r6h"]])
    bc = r6h.unsqueeze(2).broadcast_to([128, nb, 16])
    kb.op(dve, "tensor_tensor", v3(xa), v3(xa), bc, ALU.mult,
          reads=[B["xa"], B["r6h"]], writes=[B["xa"]])
    u = xa
    kb.op(dve, "tensor_scalar", qa.bitcast(I16), u, 2.0, None, ALU.mult,
          reads=[B["xa"]], writes=[B["a"]])
    kb.op(dve, "tensor_scalar", qa, qa.bitcast(I16), 4.0, 0.5, ALU.min, ALU.mult,
          reads=[B["a"]], writes=[B["a"]])
    kb.op(dve, "tensor_scalar", qb_.bitcast(I16), u, 1.0, None, ALU.mult,
          reads=[B["xa"]], writes=[B["b"]])
    kb.op(dve, "tensor_scalar", qb_, qb_.bitcast(I16), 2.0, 4.0, ALU.max, ALU.min,
          reads=[B["b"]], writes=[B["b"]])
    kb.op(dve, "tensor_scalar", qd.bitcast(I16), u, 4000.0, -20002.0, ALU.mult, ALU.add,
          reads=[B["xa"]], writes=[B["d"]])
    kb.op(dve, "tensor_scalar", qd, qd.bitcast(I16), -2.0, 0.0, ALU.max, ALU.min,
          reads=[B["d"]], writes=[B["d"]])
    kb.op(dve, "tensor_add", qa, qa, qb_, reads=[B["a"], B["b"]], writes=[B["a"]])
    kb.op(dve, "tensor_add", qa, qa, qd, reads=[B["a"], B["d"]], writes=[B["a"]])
    kb.op(act, "activation", qd, xs_ap, AF.Sign, reads=[bxs], writes=[B["d"]])
    kb.op(dve, "tensor_mul", qa, qa, qd, reads=[B["a"], B["d"]], writes=[B["a"]])
    sb_bc = sbh.unsqueeze(2).broadcast_to([128, nb, 16])
    kb.op(dve, "tensor_tensor", v3(xq_ap), v3(qa), sb_bc, ALU.mult,
          reads=[B["a"], B["sbh"]], writes=[bxq])


def build(n_seq=2, nt=16, only_pass1=False, dbg=None, no_dyadic=False, no_c0=False):
    nc = bass.Bass("TRN2", target_bir_lowering=False)
    es = ExitStack()
    kb = KB(nc, es)
    sbt = lambda name, shape, dt=BF16: es.enter_context(nc.sbuf_tensor(name, shape, dt))

    x_d = nc.dram_tensor("x", [n_seq, N, D], F32, kind="ExternalInput").ap()
    out_d = nc.dram_tensor("out", [n_seq, N, D], F32, kind="ExternalOutput").ap()
    part_d = nc.dram_tensor("part", [n_seq, N, D], F32, kind="Internal").ap()
    xnb_d = nc.dram_tensor("xnbc", [n_seq, N, D], BF16, kind="Internal").ap()
    wg_d = nc.dram_tensor("wg", [I, D], F32, kind="ExternalInput").ap()
    wu_d = nc.dram_tensor("wu", [I, D], F32, kind="ExternalInput").ap()
    wd_d = nc.dram_tensor("wd", [D, I], F32, kind="ExternalInput").ap()
    wqkv_d = nc.dram_tensor("wqkv", [3 * D, D], F32, kind="ExternalInput").ap()
    bqkv_d = nc.dram_tensor("bqkvb", [128, 3 * D], F32, kind="ExternalInput").ap()
    wgt_d = nc.dram_tensor("wgate", [D, D], F32, kind="ExternalInput").ap()
    bgt_d = nc.dram_tensor("bgateb", [128, D], F32, kind="ExternalInput").ap()
    wo_d = nc.dram_tensor("wout", [D, D], F32, kind="ExternalInput").ap()
    bo_d = nc.dram_tensor("boutb", [128, D], F32, kind="ExternalInput").ap()
    bwin_d = nc.dram_tensor("bwin", [H, WCH, 128, 128], F32, kind="ExternalInput").ap()
    bdyn_d = nc.dram_tensor("bdyn", [128, H * ND], F32, kind="ExternalInput").ap()

    B = {}
    def buf(nm):
        if nm not in B:
            B[nm] = Buf(nm)
        return B[nm]

    xnb = sbt("xnb", [128, D])
    small = [{nm: sbt(f"ln_{nm}{p}", [128, 1], F32)
              for nm in ("s1", "s2", "mean", "var", "y", "nmr", "tmp1")} for p in range(2)]
    part_t = sbt("part_t", [128, D], F32)
    ln_b = [{nm: buf(f"{nm}{p}") for nm in ("s1", "s2", "mean", "var", "y", "nmr", "tmp1", "xsq")}
            for p in range(2)]
    for p in range(2):
        buf(f"q_xa{p}")
    for nm in ("m", "mc", "r6f", "r6h", "sbh", "a", "b", "d"):
        buf(f"q_{nm}")
    b_xnb, b_part = buf("xnb"), buf("part_t")

    pe, dve, act, pool, sy = kb.pe, kb.dve, kb.act, kb.pool, kb.sy

    with ExitStack() as es1:
        s1t = lambda name, shape, dt=BF16: es1.enter_context(nc.sbuf_tensor(name, shape, dt))
        ffnw = s1t("ffnw", [128, 3 * 8 * I])
        WGO, WUO, WDO = 0, 8 * I, 16 * I
        FST = 3 * 8 * I
        wgqT = ffnw[:, WGO:WGO + 8 * I]
        wuqT = ffnw[:, WUO:WUO + 8 * I]
        wdqT = ffnw[:, WDO:WDO + 22 * D]
        b_wgqT, b_wuqT, b_wdqT = buf("wgqT"), buf("wuqT"), buf("wdqT")
        x_t = [s1t(f"x_t{p}", [128, D], F32) for p in range(2)]
        xnf = [s1t(f"xnf{p}", [128, D], F32) for p in range(2)]
        b_x = [buf(f"x_t{p}") for p in range(2)]
        b_xnf = [buf(f"xnf{p}") for p in range(2)]
        for p in range(2):
            small[p]["xsq"] = xnf[p]
            ln_b[p]["xsq"] = b_xnf[p]
        qshared = {}
        for nm in ("qa", "qb", "qd"):
            qshared[nm] = s1t(nm, [128, QW], F16)
        for nm in ("qm", "qr6f"):
            qshared[nm] = s1t(nm, [128, QW // 16], F32)
        qshared["qmc"] = qshared["qm"]
        for nm in ("qr6h", "qsbh"):
            qshared[nm] = s1t(nm, [128, QW // 16], F16)
        q_t = []
        for p in range(2):
            d = dict(qshared)
            d["qxa"] = s1t(f"qxa{p}", [128, QW], F16)
            q_t.append(d)
        xq1 = s1t("xq", [128, D])
        xq = [xq1, xq1]
        xq_fm = [s1t(f"xq_fm{p}", [128, D]) for p in range(2)]
        hq = [s1t(f"hq{p}", [128, I]) for p in range(2)]
        hq_fm = [s1t(f"hq_fm{p}", [128, I]) for p in range(2)]
        hsg1 = s1t("h_sg", [128, 512], F16)
        h_sg = [hsg1, hsg1]
        h_gt = [s1t(f"h_gt{p}", [128, 512], F16) for p in range(2)]
        hup1 = s1t("h_up", [128, 512], F16)
        h_up = [hup1, hup1]
        bias_out = s1t("bias_out", [128, D])
        b_xq1 = buf("xq")
        b_xq = [b_xq1, b_xq1]
        b_xqfm = [buf(f"xq_fm{p}") for p in range(2)]
        b_hq = [buf(f"hq{p}") for p in range(2)]
        b_hqfm = [buf(f"hq_fm{p}") for p in range(2)]
        b_hsg1 = buf("h_sg")
        b_hsg = [b_hsg1, b_hsg1]
        b_hgt = [buf(f"h_gt{p}") for p in range(2)]
        b_hup1 = buf("h_up")
        b_hup = [b_hup1, b_hup1]
        b_bo = buf("bias_out")
        xnbp = [xnb, xnb]
        b_xnbp = [b_xnb, b_xnb]
        part_p = [part_t, s1t("part2", [128, D], F32)]
        b_partp = [b_part, buf("part2")]
        wstage = x_t[0]
        b_ws = b_x[0]
        kb.dma(sy, xnf[0][:], bo_d, writes=[b_xnf[0]])
        kb.op(act, "copy", bias_out[:], xnf[0][:], reads=[b_xnf[0]], writes=[b_bo])
        for w_d, wTO, bwT, ntile, tw, ost in (
            (wg_d, WGO, b_wgqT, 22, D, I), (wu_d, WUO, b_wuqT, 22, D, I),
            (wd_d, WDO, b_wdqT, 8, I, D),
        ):
            for it in range(ntile):
                wst, bst = x_t[it % 2], b_x[it % 2]
                hqt, bhqt = hq[it % 2], b_hq[it % 2]
                for poff in range(0, tw, D):
                    cw = min(D, tw - poff)
                    kb.dma(sy, wst[:, 0:cw], w_d[it * 128:(it + 1) * 128, poff:poff + cw],
                           writes=[bst])
                    emit_quant(kb, q_t[it % 2], B, wst[:, 0:cw], hqt[:, poff:poff + cw],
                               bst, bhqt, cw, par=it % 2)
                nblk = tw // 128
                dst = bass.AP(ffnw, wTO + it * 128,
                              [[FST, 128], [ost, nblk], [1, 128]])
                src = hqt[:, 0:tw].rearrange("p (c q) -> p c q", c=nblk)
                kb.dma(act, dst, src, transpose=True, reads=[bhqt], writes=[bwT])
        pg = [es1.enter_context(nc.psum_tensor(f"pg{p}", [128, 512], F32)) for p in range(2)]
        pu = [es1.enter_context(nc.psum_tensor(f"pu{p}", [128, 512], F32)) for p in range(2)]
        pd = [es1.enter_context(nc.psum_tensor(f"pd{p}", [128, D], F32)) for p in range(2)]
        b_pg = [buf(f"pg{p}") for p in range(2)]
        b_pu = [buf(f"pu{p}") for p in range(2)]
        b_pd = [buf(f"pd{p}") for p in range(2)]

        tiles = [(it, sq) for it in range(nt) for sq in range(n_seq)]

        def p1_s0(ti):
            it, sq = tiles[ti]
            tp = ti % 2
            row = slice(it * 128, (it + 1) * 128)
            xt, bx = x_t[tp], b_x[tp]
            xnfp, bxnf = xnf[tp], b_xnf[tp]
            xqt, bxq = xq[tp], b_xq[tp]
            xqf, bxqf = xq_fm[tp], b_xqfm[tp]
            kb.dma(sy, xt[:], x_d[sq, row, :], writes=[bx])
            emit_ln(kb, small[tp], ln_b[tp], xt[:], None, xnfp[:], bx, None, bxnf)
            kb.dma(pool, xnb_d[sq, row, :], xnfp[:], reads=[bxnf], writes=[buf("xnb_d")])
            emit_quant(kb, q_t[tp], B, xnfp[:], xqt[:], bxnf, bxq, D, par=tp)
            kb.dma(act, xqf[:].rearrange("p (c q) -> p c q", c=8),
                   xqt[:].rearrange("p (c q) -> p c q", c=8),
                   transpose=True, reads=[bxq], writes=[bxqf])

        def p1_s1(ti):
            it, sq = tiles[ti]
            tp = ti % 2
            xqf, bxqf = xq_fm[tp], b_xqfm[tp]
            hqt, bhq = hq[tp], b_hq[tp]
            hqf, bhqf = hq_fm[tp], b_hqfm[tp]
            hfl_ap = hqf[:].bitcast(F16)
            for ci, (off, cw) in enumerate(ICHUNKS):
                cp = ci % 2
                for dc in range(8):
                    lhsT = xqf[:, dc * 128:(dc + 1) * 128]
                    kb.op(pe, "matmul", pg[cp][:, 0:cw], lhsT,
                          wgqT[:, dc * I + off: dc * I + off + cw],
                          start=(dc == 0), stop=(dc == 7),
                          reads=[bxqf, b_wgqT], writes=[b_pg[cp]])
                    kb.op(pe, "matmul", pu[cp][:, 0:cw], lhsT,
                          wuqT[:, dc * I + off: dc * I + off + cw],
                          start=(dc == 0), stop=(dc == 7),
                          reads=[bxqf, b_wuqT], writes=[b_pu[cp]])
                kb.op(act, "activation", h_sg[tp][:, 0:cw], pg[cp][:, 0:cw], AF.Tanh,
                      scale=0.5, reads=[b_pg[cp]], writes=[b_hsg[tp]])
                kb.op(act, "copy", h_gt[tp][:, 0:cw], pg[cp][:, 0:cw],
                      reads=[b_pg[cp]], writes=[b_hgt[tp]])
                kb.op(act, "copy", h_up[tp][:, 0:cw], pu[cp][:, 0:cw],
                      reads=[b_pu[cp]], writes=[b_hup[tp]])
                kb.op(dve, "tensor_scalar", h_sg[tp][:, 0:cw], h_sg[tp][:, 0:cw], 0.5, 0.5,
                      ALU.mult, ALU.add, reads=[b_hsg[tp]], writes=[b_hsg[tp]])
                kb.op(dve, "tensor_mul", h_gt[tp][:, 0:cw], h_gt[tp][:, 0:cw], h_up[tp][:, 0:cw],
                      reads=[b_hgt[tp], b_hup[tp]], writes=[b_hgt[tp]])
                kb.op(dve, "tensor_mul", hfl_ap[:, off:off + cw], h_gt[tp][:, 0:cw],
                      h_sg[tp][:, 0:cw], reads=[b_hgt[tp], b_hsg[tp]], writes=[bhqf])
            for qoff in range(0, I, QW):
                cw = min(QW, I - qoff)
                emit_quant(kb, q_t[tp], B, hfl_ap[:, qoff:qoff + cw],
                           hqt[:, qoff:qoff + cw], bhqf, bhq, cw, par=tp)
            kb.dma(act, hqf[:].rearrange("p (c q) -> p c q", c=22),
                   hqt[:].rearrange("p (c q) -> p c q", c=22),
                   transpose=True, reads=[bhq], writes=[bhqf])

        def p1_s2(ti):
            it, sq = tiles[ti]
            tp = ti % 2
            row = slice(it * 128, (it + 1) * 128)
            xt, bx = x_t[tp], b_x[tp]
            hqf, bhqf = hq_fm[tp], b_hqfm[tp]
            ptt, bpt = part_p[tp], b_partp[tp]
            for half in range(2):
                for ic in range(22):
                    kb.op(pe, "matmul", pd[tp][:, half * 512:(half + 1) * 512],
                          hqf[:, ic * 128:(ic + 1) * 128],
                          wdqT[:, ic * D + half * 512: ic * D + (half + 1) * 512],
                          start=(ic == 0), stop=(ic == 21),
                          reads=[bhqf, b_wdqT], writes=[b_pd[tp]])
            kb.op(dve, "tensor_add", ptt[:], pd[tp][:], xt[:],
                  reads=[b_pd[tp], bx], writes=[bpt])
            kb.op(dve, "tensor_add", ptt[:], ptt[:], bias_out[:],
                  reads=[bpt, b_bo], writes=[bpt])
            dst = (out_d if only_pass1 else part_d)[sq, row, :]
            kb.dma(sy, dst, ptt[:], reads=[bpt],
                   writes=([] if only_pass1 else [buf("part_d")]))

        T = len(tiles)
        for k in range(T + 2):
            if 0 <= k - 2 < T:
                p1_s2(k - 2)
            if 0 <= k - 1 < T:
                p1_s1(k - 1)
            if k < T:
                p1_s0(k)

    if only_pass1:
        kb.finalize()
        es.close()
        return nc

    env = dict(locals()); env['sbt'] = sbt; env['dbg'] = dbg
    env['no_dyadic'] = no_dyadic; env['no_c0'] = no_c0
    build_pass2(nc, es, kb, B, buf, env)
    kb.finalize()
    es.close()
    return nc


def build_pass2(nc, es, kb, B, buf, env):
    sbt = env["sbt"]
    small = env["small"]
    xnb, part_t = env["xnb"], env["part_t"]
    x_d, out_d, part_d, xnb_d = env["x_d"], env["out_d"], env["part_d"], env["xnb_d"]
    wqkv_d, bqkv_d = env["wqkv_d"], env["bqkv_d"]
    wgt_d, bgt_d, wo_d = env["wgt_d"], env["bgt_d"], env["wo_d"]
    bwin_d, bdyn_d = env["bwin_d"], env["bdyn_d"]
    n_seq, nt = env["n_seq"], env["nt"]
    no_dyadic = env.get("no_dyadic"); no_c0 = env.get("no_c0")
    pe, dve, act, pool, sy = kb.pe, kb.dve, kb.act, kb.pool, kb.sy
    b_partd, b_xnbd = buf("part_d"), buf("xnb_d")
    NEG = -1e30

    with ExitStack() as es2:
        s2t = lambda name, shape, dt=BF16: es2.enter_context(nc.sbuf_tensor(name, shape, dt))
        attnw = s2t("attnw", [128, 8 * 3 * D + 16 * D])
        ATST = 8 * 3 * D + 16 * D
        WQO, WGO2, WOO = 0, 8 * 3 * D, 8 * 3 * D + 8 * D
        wqkvT = attnw[:, WQO:WQO + 24 * D]
        wgT = attnw[:, WGO2:WGO2 + 8 * D]
        woT = attnw[:, WOO:WOO + 8 * D]
        b_wq, b_wg2, b_wo = buf("wqkvT"), buf("wgT"), buf("woT")
        k_tm = s2t("k_tm", [128, 16 * D])
        v_tm = s2t("v_tm", [128, 16 * D])
        kfm = s2t("kfm", [128, 3 * D])
        ebw = s2t("ebw", [128, H * WCH * 128])
        ebd = s2t("ebd", [128, H * ND])
        bias_qkv = s2t("bias_qkv", [128, 3 * D])
        bias_gate = s2t("bias_gate", [128, D])
        ones_col = s2t("ones_col", [128, 1])
        qtm1 = s2t("q_tm", [128, D])
        q_tm = [qtm1, qtm1]
        q_fm = [s2t(f"q_fm{p}", [128, D]) for p in range(2)]
        e4 = [s2t(f"e4{p}", [128, 4 * WCH * 128]) for p in range(2)]
        sdyn = s2t("sdyn", [128, H * ND], F32)
        edyn = s2t("edyn", [128, H * ND])
        dsum = s2t("dsum", [128, H], F32)
        ssum = [s2t(f"ssum{p}", [128, H], F32) for p in range(2)]
        srec = [s2t(f"srec{p}", [128, H], F32) for p in range(2)]
        g1 = s2t("g1", [128, D])
        g21 = s2t("g2", [128, D])
        g2 = [g21, g21]
        g41 = s2t("g4", [128, D])
        g4 = [g41, g41]
        flat_sb = [s2t(f"flat_sb{p}", [128, D]) for p in range(2)]
        fg1t = s2t("fg", [128, D])
        fg = [fg1t, fg1t]
        fgf1 = s2t("fg_fm", [128, D])
        fg_fm = [fgf1, fgf1]
        xnb2p = [xnb, xnb]
        xnbfm2 = [s2t(f"xnbfm2{p}", [128, D]) for p in range(2)]
        part2p = [part_t, part_t]
        b_part2p = [B["part_t"], B["part_t"]]
        wstage2 = part_t
        b_ws2 = B["part_t"]
        b_ktm = [buf(f"k_tm{j}") for j in range(16)]
        b_vtm = [buf(f"v_tm{j}") for j in range(16)]
        b_kfm = [buf(f"kfm{j}") for j in range(3)]
        b_ebw, b_ebd = buf("ebw"), buf("ebd")
        b_bq, b_bg = buf("bias_qkv"), buf("bias_gate")
        b_ones = buf("ones_col")
        b_qtm1 = buf("q_tm")
        b_qtm = [b_qtm1, b_qtm1]
        b_qfm = [buf(f"q_fm{p}") for p in range(2)]
        b_e4 = [buf(f"e4{p}") for p in range(2)]
        b_sdyn, b_edyn = buf("sdyn"), buf("edyn")
        b_dsum = buf("dsum")
        b_ssum = [buf(f"ssum{p}") for p in range(2)]
        b_srec = [buf(f"srec{p}") for p in range(2)]
        b_g1 = buf("g1")
        b_g21 = buf("g2")
        b_g2 = [b_g21, b_g21]
        b_g41 = buf("g4")
        b_g4 = [b_g41, b_g41]
        b_flat = [buf(f"flat_sb{p}") for p in range(2)]
        b_fg1 = buf("fg")
        b_fg = [b_fg1, b_fg1]
        b_fgf1 = buf("fg_fm")
        b_fgfm = [b_fgf1, b_fgf1]
        b_xnb2p = [B["xnb"], B["xnb"]]
        b_xnbfm2 = [buf(f"xnbfm2{p}") for p in range(2)]

        P1 = es2.enter_context(nc.psum_tensor("P1", [128, 2048], F32))
        P2 = es2.enter_context(nc.psum_tensor("P2", [128, D], F32))
        P3 = es2.enter_context(nc.psum_tensor("P3", [128, 512], F32))
        P4 = es2.enter_context(nc.psum_tensor("P4", [128, 512], F32))
        b_P1b = [buf(f"P1b{b}") for b in range(4)]
        b_P2, b_P3, b_P4 = buf("P2"), buf("P3"), buf("P4")
        qkv_slots = [(P1[:, 0:512], b_P1b[0]), (P1[:, 512:1024], b_P1b[1]),
                     (P4[:, 0:512], b_P4)]

        kb.op(dve, "memset", ones_col[:], 1.0, writes=[b_ones])
        kb.op(dve, "memset", P1[:], 0.0, writes=b_P1b)
        kb.op(dve, "memset", P2[:], 0.0, writes=[b_P2])
        kb.op(dve, "memset", P3[:], 0.0, writes=[b_P3])
        kb.op(dve, "memset", P4[:], 0.0, writes=[b_P4])
        wcv = [(xnb, B["xnb"]), (q_tm[0], b_qtm[0])]
        for it in range(24):
            cv, bcv = wcv[it % 2]
            if it < 8:
                kb.dma(sy, wstage2[:], wqkv_d[it * 128:(it + 1) * 128, :], writes=[b_ws2])
                kb.op(act, "activation", cv[:], wstage2[:], AF.Identity, scale=0.125,
                      reads=[b_ws2], writes=[bcv])
            else:
                kb.dma(pool, cv[:], wqkv_d[it * 128:(it + 1) * 128, :], writes=[bcv])
            dst = bass.AP(attnw, WQO + it * 128, [[ATST, 128], [3 * D, 8], [1, 128]])
            kb.dma(act, dst, cv[:].rearrange("p (c q) -> p c q", c=8),
                   transpose=True, reads=[bcv], writes=[b_wq])
        for wi, (w_d, base, bw) in enumerate(((wgt_d, WGO2, b_wg2), (wo_d, WOO, b_wo))):
            for it in range(8):
                cv, bcv = wcv[it % 2]
                kb.dma(pool, cv[:], w_d[it * 128:(it + 1) * 128, :], writes=[bcv])
                dst = bass.AP(attnw, base + it * 128, [[ATST, 128], [D, 8], [1, 128]])
                kb.dma(act, dst, cv[:].rearrange("p (c q) -> p c q", c=8),
                       transpose=True, reads=[bcv], writes=[bw])
        for c in range(3):
            kb.dma(sy, wstage2[:, 0:D], bqkv_d[:, c * D:(c + 1) * D], writes=[b_ws2])
            kb.op(act, "activation", bias_qkv[:, c * D:(c + 1) * D], wstage2[:, 0:D],
                  AF.Identity, scale=(0.125 if c == 0 else 1.0),
                  reads=[b_ws2], writes=[b_bq])
        kb.dma(sy, wstage2[:, 0:D], bgt_d, writes=[b_ws2])
        kb.op(act, "copy", bias_gate[:], wstage2[:, 0:D], reads=[b_ws2], writes=[b_bg])
        for ch in range(6):
            src = bass.AP(bwin_d.tensor, ch * 8 * 128 * 128,
                          [[128, 128], [128 * 128, 8], [1, 128]])
            kb.dma(sy, wstage2[:, 0:1024], src, writes=[b_ws2])
            kb.op(act, "activation", ebw[:, ch * 1024:(ch + 1) * 1024],
                  wstage2[:, 0:1024], AF.Exp, reads=[b_ws2], writes=[b_ebw])
        kb.dma(sy, wstage2[:, 0:H * ND], bdyn_d, writes=[b_ws2])
        kb.op(act, "activation", ebd[:], wstage2[:, 0:H * ND], AF.Exp,
              reads=[b_ws2], writes=[b_ebd])

        tiles2 = [(sq, it) for sq in range(n_seq) for it in range(nt)]
        qkv_slots = [(P3[:, 0:512], b_P3), (P4[:, 0:512], b_P4)]

        def p2_t0(ti):
            sq, it = tiles2[ti]
            tp = ti % 2
            row = slice(it * 128, (it + 1) * 128)
            xnbt, bxnbt = xnb2p[tp], b_xnb2p[tp]
            xfm, bxfm = xnbfm2[tp], b_xnbfm2[tp]
            qtm, bqtm = q_tm[tp], b_qtm[tp]
            qfm, bqfm = q_fm[tp], b_qfm[tp]
            kb.dma(sy, xnbt[:], xnb_d[sq, row, :], reads=[b_xnbd], writes=[bxnbt])
            kb.dma(act, xfm[:].rearrange("p (c q) -> p c q", c=8),
                   xnbt[:].rearrange("p (c q) -> p c q", c=8),
                   transpose=True, reads=[bxnbt], writes=[bxfm])
            for g in range(6):
                ps_g, b_ps = qkv_slots[g % 2]
                for dc in range(8):
                    kb.op(pe, "matmul", ps_g, xfm[:, dc * 128:(dc + 1) * 128],
                          wqkvT[:, dc * 3 * D + g * 512: dc * 3 * D + (g + 1) * 512],
                          start=(dc == 0), stop=(dc == 7),
                          reads=[bxfm, b_wq], writes=[b_ps])
                if g < 2:
                    dstap, dbuf = qtm[:, g * 512:(g + 1) * 512], bqtm
                elif g < 4:
                    dstap = k_tm[:, it * D + (g - 2) * 512: it * D + (g - 1) * 512]
                    dbuf = b_ktm[it]
                else:
                    dstap = v_tm[:, it * D + (g - 4) * 512: it * D + (g - 3) * 512]
                    dbuf = b_vtm[it]
                kb.op(dve, "scalar_tensor_tensor", dstap, ps_g, 1.0,
                      bias_qkv[:, g * 512:(g + 1) * 512], ALU.mult, ALU.add,
                      reads=[b_ps, b_bq], writes=[dbuf])
            kb.dma(act, qfm[:].rearrange("p (c q) -> p c q", c=8),
                   qtm[:].rearrange("p (c q) -> p c q", c=8),
                   transpose=True, reads=[bqtm], writes=[bqfm])
            kb.dma(act, kfm[:, (it % 3) * D:(it % 3 + 1) * D].rearrange("p (c q) -> p c q", c=8),
                   k_tm[:, it * D:(it + 1) * D].rearrange("p (c q) -> p c q", c=8),
                   transpose=True, reads=[b_ktm[it]], writes=[b_kfm[it % 3]])

        def p2_t1(ti):
            sq, it = tiles2[ti]
            tp = ti % 2
            qtm, bqtm = q_tm[tp], b_qtm[tp]
            qfm, bqfm = q_fm[tp], b_qfm[tp]
            kb.op(dve, "memset", sdyn[:], NEG, writes=[b_sdyn])
            for mi, o in enumerate(DYADIC):
                ot = o // 128
                if it < ot or no_dyadic:
                    continue
                kb.op(dve, "tensor_mul", g1[:], qtm[:],
                      k_tm[:, (it - ot) * D:(it - ot + 1) * D],
                      reads=[bqtm, b_ktm[it - ot]], writes=[b_g1])
                kb.op(dve, "reduce_sum", sdyn[:, mi * H:(mi + 1) * H],
                      g1[:].rearrange("p (h d) -> p h d", d=HD), AX.X,
                      reads=[b_g1], writes=[b_sdyn])
            kb.op(act, "activation", edyn[:], sdyn[:], AF.Exp,
                  reads=[b_sdyn], writes=[b_edyn])
            kb.op(dve, "tensor_mul", edyn[:], edyn[:], ebd[:],
                  reads=[b_edyn, b_ebd], writes=[b_edyn])
            ev = bass.AP(edyn, 0, [[edyn[:].ap[0][0], 128], [1, H], [H, ND]])
            kb.op(dve, "reduce_sum", dsum[:], ev, AX.X,
                  reads=[b_edyn], writes=[b_dsum])
            for hg in range(4):
                ep = hg % 2
                heads = range(4 * hg, 4 * hg + 4)
                for hh, h in sorted(enumerate(heads), key=lambda p: p[1] % 2):
                    for c in range(WCH):
                        jt = it - 2 + c
                        reg = P1[:, hh * 512 + c * 128: hh * 512 + (c + 1) * 128]
                        if jt < 0 or (no_c0 and c == 0):
                            kb.op(dve, "memset", reg, NEG, writes=[b_P1b[hh]])
                            continue
                        pb = (h % 2) * 64
                        lhsT = kfm[pb:pb + 64,
                                   (jt % 3) * D + (h // 2) * 128:(jt % 3) * D + (h // 2) * 128 + 128]
                        rhs = qfm[pb:pb + 64, (h // 2) * 128:(h // 2) * 128 + 128]
                        kb.op(pe, "matmul", reg, lhsT, rhs, start=True, stop=True,
                              reads=[b_kfm[jt % 3], bqfm], writes=[b_P1b[hh]])
                pv = P1[:].rearrange("p (a r) -> p a r", r=512)[:, :, 0:WCH * 128]
                ev4 = e4[ep][:].rearrange("p (a r) -> p a r", r=WCH * 128)
                kb.op(act, "activation", ev4, pv, AF.Exp,
                      reads=b_P1b, writes=[b_e4[ep]])
                ebv = ebw[:, (4 * hg) * WCH * 128:(4 * hg + 4) * WCH * 128]
                kb.op(dve, "tensor_mul", e4[ep][:], e4[ep][:], ebv,
                      reads=[b_e4[ep], b_ebw], writes=[b_e4[ep]])
                for hh, h in enumerate(heads):
                    valid = [c for c in range(WCH)
                             if it - 2 + c >= 0 and not (no_c0 and c == 0)]
                    for vi, c in enumerate(valid):
                        jt = it - 2 + c
                        ech = e4[ep][:, hh * 384 + c * 128: hh * 384 + (c + 1) * 128]
                        kb.op(pe, "matmul", P1[:, 1920 + h: 1921 + h], ech, ones_col[:],
                              start=(vi == 0), stop=(vi == len(valid) - 1),
                              skip_group_check=True,
                              reads=[b_e4[ep], b_ones], writes=[b_P1b[3]])
                        kb.op(pe, "matmul", P2[:, h * HD:(h + 1) * HD], ech,
                              v_tm[:, jt * D + h * HD: jt * D + (h + 1) * HD],
                              start=(vi == 0), stop=(vi == len(valid) - 1),
                              skip_group_check=True,
                              reads=[b_e4[ep], b_vtm[jt]], writes=[b_P2])
            kb.op(dve, "tensor_add", ssum[tp][:], P1[:, 1920:1920 + H], dsum[:],
                  reads=[b_P1b[3], b_dsum], writes=[b_ssum[tp]])
            kb.op(dve, "reciprocal", srec[tp][:], ssum[tp][:],
                  reads=[b_ssum[tp]], writes=[b_srec[tp]])
            vd = [] if no_dyadic else [o for o in DYADIC if it >= o // 128]
            if not vd:
                kb.op(dve, "memset", g2[tp][:], 0.0, writes=[b_g2[tp]])
            for vi, o in enumerate(vd):
                mi = DYADIC.index(o)
                e_b = edyn[:, mi * H:(mi + 1) * H]
                e_bc = e_b.unsqueeze(2).broadcast_to([128, H, HD])
                tgt, tb = (g2[tp], b_g2[tp]) if vi == 0 else (g1, b_g1)
                kb.op(dve, "tensor_tensor",
                      tgt[:].rearrange("p (h d) -> p h d", d=HD),
                      v_tm[:, (it - o // 128) * D:(it - o // 128 + 1) * D]
                      .rearrange("p (h d) -> p h d", d=HD),
                      e_bc, ALU.mult,
                      reads=[b_vtm[it - o // 128], b_edyn], writes=[tb])
                if vi > 0:
                    kb.op(dve, "tensor_add", g2[tp][:], g2[tp][:], g1[:],
                          reads=[b_g2[tp], b_g1], writes=[b_g2[tp]])
            srec_bc = srec[tp][:].unsqueeze(2).broadcast_to([128, H, HD])
            v3h = lambda ap: ap.rearrange("p (h d) -> p h d", d=HD)
            kb.op(dve, "tensor_tensor", v3h(flat_sb[tp][:]), v3h(P2[:]), srec_bc,
                  ALU.mult, reads=[b_P2, b_srec[tp]], writes=[b_flat[tp]])
            kb.op(dve, "tensor_tensor", v3h(g2[tp][:]), v3h(g2[tp][:]), srec_bc,
                  ALU.mult, reads=[b_g2[tp], b_srec[tp]], writes=[b_g2[tp]])
            kb.op(dve, "tensor_add", flat_sb[tp][:], flat_sb[tp][:], g2[tp][:],
                  reads=[b_flat[tp], b_g2[tp]], writes=[b_flat[tp]])

        def p2_t2(ti):
            sq, it = tiles2[ti]
            tp = ti % 2
            row = slice(it * 128, (it + 1) * 128)
            xfm, bxfm = xnbfm2[tp], b_xnbfm2[tp]
            ptt, bpt = part2p[tp], b_part2p[tp]
            kb.dma(sy, ptt[:], part_d[sq, row, :], reads=[b_partd], writes=[bpt])
            for g in range(2):
                for dc in range(8):
                    kb.op(pe, "matmul", P1[:, g * 512:(g + 1) * 512],
                          xfm[:, dc * 128:(dc + 1) * 128],
                          wgT[:, dc * D + g * 512: dc * D + (g + 1) * 512],
                          start=(dc == 0), stop=(dc == 7),
                          reads=[bxfm, b_wg2], writes=[b_P1b[g]])
            kb.op(dve, "tensor_add", g4[tp][:], P1[:, 0:D], bias_gate[:],
                  reads=[b_P1b[0], b_P1b[1], b_bg], writes=[b_g4[tp]])
            kb.op(act, "activation", g4[tp][:], g4[tp][:], AF.Tanh, scale=0.5,
                  reads=[b_g4[tp]], writes=[b_g4[tp]])
            kb.op(dve, "tensor_scalar", g4[tp][:], g4[tp][:], 0.5, 0.5,
                  ALU.mult, ALU.add, reads=[b_g4[tp]], writes=[b_g4[tp]])
            kb.op(dve, "tensor_mul", fg[tp][:], flat_sb[tp][:], g4[tp][:],
                  reads=[b_flat[tp], b_g4[tp]], writes=[b_fg[tp]])
            kb.dma(act, fg_fm[tp][:].rearrange("p (c q) -> p c q", c=8),
                   fg[tp][:].rearrange("p (c q) -> p c q", c=8),
                   transpose=True, reads=[b_fg[tp]], writes=[b_fgfm[tp]])
            for g in range(2):
                for dc in range(8):
                    kb.op(pe, "matmul", P2[:, g * 512:(g + 1) * 512],
                          fg_fm[tp][:, dc * 128:(dc + 1) * 128],
                          woT[:, dc * D + g * 512: dc * D + (g + 1) * 512],
                          start=(dc == 0), stop=(dc == 7),
                          reads=[b_fgfm[tp], b_wo], writes=[b_P2])
            kb.op(dve, "tensor_add", ptt[:], P2[:], ptt[:],
                  reads=[b_P2, bpt], writes=[bpt])
            kb.dma(sy, out_d[sq, row, :], ptt[:], reads=[bpt])

        T2n = len(tiles2)
        for k in range(T2n + 2):
            if 0 <= k - 2 < T2n:
                p2_t2(k - 2)
            if 0 <= k - 1 < T2n:
                p2_t1(k - 1)
            if k < T2n:
                p2_t0(k)


_NC_CACHE = {}


def kernel(**inputs):
    maps = core_inputs(inputs, n_cores=8)
    if "nc" not in _NC_CACHE:
        _NC_CACHE["nc"] = build(n_seq=2, nt=16, only_pass1=False)
    nc = _NC_CACHE["nc"]
    res = run_bass_kernel_spmd(nc, maps, core_ids=list(range(8)))
    out = np.concatenate([np.asarray(r["out"]) for r in res.results], axis=0)
    return out.astype(np.float32)


# version 07
# speedup vs baseline: 1.3701x; 1.0164x over previous
import numpy as np
from contextlib import ExitStack

import concourse.bass as bass
import concourse.mybir as mybir
from concourse.bass_utils import run_bass_kernel_spmd

"""Host-side input prep shared by dev scripts and kernel.py."""

_DENSE_LOCAL_W = 64
_DYADIC_ALL = [96, 128, 192, 256, 384, 512, 768, 1024, 1536]
OFFSETS = sorted(set(range(0, _DENSE_LOCAL_W + 1)) | set(_DYADIC_ALL))
WIN_OFF = set(range(0, 65)) | {96, 128, 192}
DYADIC = [256, 384, 512, 768, 1024, 1536]
H = 16
NEG = -1e30


def make_bwin(pos_bias):
    pb = {o: pos_bias[m] for m, o in enumerate(OFFSETS)}
    out = np.full((H, 3, 128, 128), NEG, dtype=np.float32)
    for c in range(3):
        base = (2 - c) * 128
        for j in range(128):
            for i in range(128):
                off = base + i - j
                if off in WIN_OFF:
                    out[:, c, j, i] = pb[off]
    return out


def make_bdyn(pos_bias):
    idx = {o: m for m, o in enumerate(OFFSETS)}
    v = np.empty((len(DYADIC), H), dtype=np.float32)
    for mi, o in enumerate(DYADIC):
        v[mi, :] = pos_bias[idx[o]]
    row = v.reshape(-1)
    return np.broadcast_to(row, (128, H * len(DYADIC))).copy()


def core_inputs(ins, n_cores=8):
    x = np.ascontiguousarray(ins["L0_attn_x"], dtype=np.float32)
    bwin = np.ascontiguousarray(make_bwin(np.asarray(ins["L0_attn_pos_bias"])))
    bdyn = np.ascontiguousarray(make_bdyn(np.asarray(ins["L0_attn_pos_bias"])))
    com = {
        "wg": np.ascontiguousarray(ins["L0_ffn_w_gate"], dtype=np.float32),
        "wu": np.ascontiguousarray(ins["L0_ffn_w_up"], dtype=np.float32),
        "wd": np.ascontiguousarray(ins["L0_ffn_w_down"], dtype=np.float32),
        "wqkv": np.ascontiguousarray(ins["L0_attn_Wqkv"], dtype=np.float32),
        "bqkvb": np.ascontiguousarray(
            np.broadcast_to(ins["L0_attn_bqkv"], (128, 3072))).astype(np.float32),
        "wgate": np.ascontiguousarray(ins["L0_attn_Wgate"], dtype=np.float32),
        "bgateb": np.ascontiguousarray(
            np.broadcast_to(ins["L0_attn_bgate"], (128, 1024))).astype(np.float32),
        "wout": np.ascontiguousarray(ins["L0_attn_Wout"], dtype=np.float32),
        "boutb": np.ascontiguousarray(
            np.broadcast_to(ins["L0_attn_bout"], (128, 1024))).astype(np.float32),
        "bwin": bwin,
        "bdyn": bdyn,
    }
    per = x.shape[0] // n_cores
    maps = []
    for c in range(n_cores):
        m = dict(com)
        m["x"] = np.ascontiguousarray(x[c * per:(c + 1) * per])
        maps.append(m)
    return maps


"""Kernel-builder helpers: per-engine semaphore bookkeeping + dependency tracking.

Compute instructions increment their engine's progress semaphore (+1 each).
DMA queues use R rotating semaphores with issue-gating: a DMA on rotation
slot j may only issue once the previous DMA on slot j completed, so each
slot's semaphore is serialized and every 16*m boundary is a deterministic
wait value (concurrent DMAs on ONE semaphore post their 16 increments
interleaved, making intermediate waits racy).
"""

NDMASEM = 6


class Buf:
    __slots__ = ("name", "writer", "readers")

    def __init__(self, name):
        self.name = name
        self.writer = None
        self.readers = []


class Eng:
    def __init__(self, kb, name, eng, sem, dma_sems):
        self.kb = kb
        self.name = name
        self.e = eng
        self.sem = sem
        self.dma_sems = dma_sems
        self.dma_counts = [0] * len(dma_sems)
        self.dma_idx = 0
        self.n = 0
        self.known = {}

    def _wait(self, token):
        sem, val = token
        if self.name == "pe" and sem is self.sem:
            return
        if self.known.get(id(sem), -1) >= val:
            return
        self.e.wait_ge(sem, val)
        self.known[id(sem)] = val


class KB:
    def __init__(self, nc, es):
        self.nc = nc
        mk = lambda nm: es.enter_context(nc.semaphore(nm))
        self.pe = Eng(self, "pe", nc.tensor, mk("sem_pe"), [])
        self.dve = Eng(self, "dve", nc.vector, mk("sem_dve"), [])
        self.act = Eng(self, "act", nc.scalar, mk("sem_act"),
                       [mk(f"dsc{i}") for i in range(NDMASEM)])
        self.pool = Eng(self, "pool", nc.gpsimd, mk("sem_pool"),
                        [mk(f"dg{i}") for i in range(NDMASEM)])
        self.sy = Eng(self, "sy", nc.sync, mk("sem_sy"),
                      [mk(f"dsy{i}") for i in range(NDMASEM)])
        self.engines = [self.pe, self.dve, self.act, self.pool, self.sy]

    def op(self, eng, method, *args, reads=(), writes=(), **kwargs):
        for b in reads:
            if b.writer is not None:
                eng._wait(b.writer)
        for b in writes:
            if b.writer is not None:
                eng._wait(b.writer)
            for t in b.readers:
                eng._wait(t)
        inst = getattr(eng.e, method)(*args, **kwargs)
        eng.n += 1
        inst.then_inc(eng.sem, 1)
        tok = (eng.sem, eng.n)
        for b in reads:
            b.readers.append(tok)
        for b in writes:
            b.writer = tok
            b.readers = []
        return tok

    def dma(self, eng, dst, src, reads=(), writes=(), transpose=False, **kw):
        for b in reads:
            if b.writer is not None:
                eng._wait(b.writer)
        for b in writes:
            if b.writer is not None:
                eng._wait(b.writer)
            for t in b.readers:
                eng._wait(t)
        j = eng.dma_idx % len(eng.dma_sems)
        eng.dma_idx += 1
        sem = eng.dma_sems[j]
        if eng.dma_counts[j] > 0:
            eng._wait((sem, 16 * eng.dma_counts[j]))
        if transpose:
            inst = eng.e.dma_start_transpose(dst, src)
        else:
            inst = eng.e.dma_start(dst, src, **kw)
        eng.dma_counts[j] += 1
        inst.then_inc(sem, 16)
        tok = (sem, 16 * eng.dma_counts[j])
        for b in reads:
            b.readers.append(tok)
        for b in writes:
            b.writer = tok
            b.readers = []
        return tok

    def barrier_token(self, eng):
        return (eng.sem, eng.n)

    def finalize(self):
        for e in self.engines:
            for j, sem in enumerate(e.dma_sems):
                if e.dma_counts[j] > 0:
                    e._wait((sem, 16 * e.dma_counts[j]))


def full_barrier(kb):
    toks = []
    for e in kb.engines:
        if e.n > 0:
            toks.append((e.sem, e.n))
        for j, sem in enumerate(e.dma_sems):
            if e.dma_counts[j] > 0:
                toks.append((sem, 16 * e.dma_counts[j]))
    for e in kb.engines:
        for t in toks:
            e._wait(t)


"""Megakernel builder for mk_prb_d1: out = x + attn(LN(x)) + ffn(LN(x)).

Per core: n_seq sequences of 2048 tokens. Two weight-phased passes inside ONE
program: pass1 = FFN (fp4 SwiGLU) writing partial = x + ffn + bout to DRAM
scratch (plus xn bf16 cache); pass2 = sparse attention reading partial/xn and
writing out.
"""

F32 = mybir.dt.float32
BF16 = mybir.dt.bfloat16
F16 = mybir.dt.float16
ALU = mybir.AluOpType
AX = mybir.AxisListType
AF = mybir.ActivationFunctionType

D = 1024
N = 2048
I = 2816
ICHUNKS = [(0, 512), (512, 512), (1024, 512), (1536, 512), (2048, 512), (2560, 256)]
H = 16
HD = 64
WCH = 3
DYADIC = [256, 384, 512, 768, 1024, 1536]
ND = len(DYADIC)
EPS = 1e-5
QW = 1408


def emit_ln(kb, t, b, x_ap, xnb_ap, xnf_ap, bx, bxnb, bxnf):
    dve = kb.dve
    kb.op(dve, "reduce_sum", t["s1"][:], x_ap, AX.X, reads=[bx], writes=[b["s1"]])
    kb.op(dve, "tensor_mul", t["xsq"][:], x_ap, x_ap, reads=[bx], writes=[b["xsq"]])
    kb.op(dve, "reduce_sum", t["s2"][:], t["xsq"][:], AX.X, reads=[b["xsq"]], writes=[b["s2"]])
    kb.op(dve, "tensor_scalar", t["mean"][:], t["s1"][:], 1.0 / D, None, ALU.mult,
          reads=[b["s1"]], writes=[b["mean"]])
    kb.op(dve, "tensor_mul", t["tmp1"][:], t["mean"][:], t["mean"][:],
          reads=[b["mean"]], writes=[b["tmp1"]])
    kb.op(dve, "scalar_tensor_tensor", t["var"][:], t["s2"][:], 1.0 / D, t["tmp1"][:],
          ALU.mult, ALU.subtract, reads=[b["s2"], b["tmp1"]], writes=[b["var"]])
    kb.op(dve, "tensor_scalar", t["var"][:], t["var"][:], 1.0, EPS, ALU.mult, ALU.add,
          reads=[b["var"]], writes=[b["var"]])
    kb.op(dve, "tensor_scalar", t["y"][:], t["var"][:], -0.5, 1.5, ALU.mult, ALU.add,
          reads=[b["var"]], writes=[b["y"]])
    for _ in range(2):
        kb.op(dve, "tensor_mul", t["tmp1"][:], t["y"][:], t["y"][:],
              reads=[b["y"]], writes=[b["tmp1"]])
        kb.op(dve, "tensor_mul", t["tmp1"][:], t["tmp1"][:], t["var"][:],
              reads=[b["tmp1"], b["var"]], writes=[b["tmp1"]])
        kb.op(dve, "tensor_scalar", t["tmp1"][:], t["tmp1"][:], -0.5, 1.5, ALU.mult, ALU.add,
              reads=[b["tmp1"]], writes=[b["tmp1"]])
        kb.op(dve, "tensor_mul", t["y"][:], t["y"][:], t["tmp1"][:],
              reads=[b["y"], b["tmp1"]], writes=[b["y"]])
    kb.op(dve, "scalar_tensor_tensor", t["nmr"][:], t["mean"][:], -1.0, t["y"][:],
          ALU.mult, ALU.mult, reads=[b["mean"], b["y"]], writes=[b["nmr"]])
    kb.op(dve, "tensor_scalar", xnf_ap, x_ap, t["y"][:], t["nmr"][:], ALU.mult, ALU.add,
          reads=[bx, b["y"], b["nmr"]], writes=[bxnf])


def emit_quant(kb, t, b, xs_ap, xq_ap, bxs, bxq, width, par=0):
    dve, act = kb.dve, kb.act
    nb = width // 16
    W = width
    I16 = mybir.dt.int16
    v3 = lambda ap: ap.rearrange("p (a c) -> p a c", c=16)
    xa, qa, qb_, qd = (t["qxa"][:, :W], t["qa"][:, :W], t["qb"][:, :W], t["qd"][:, :W])
    m, mc, r6f = t["qm"][:, :nb], t["qmc"][:, :nb], t["qr6f"][:, :nb]
    r6h, sbh = t["qr6h"][:, :nb], t["qsbh"][:, :nb]
    B = {k: b[f"q_{k}"] for k in ("m", "mc", "r6f", "r6h", "sbh", "a", "b", "d")}
    B["xa"] = b[f"q_xa{par}"]
    B["mc"] = B["m"]
    kb.op(act, "activation", xa, xs_ap, AF.Abs, reads=[bxs], writes=[B["xa"]])
    kb.op(dve, "reduce_max", m, v3(xa), AX.X, reads=[B["xa"]], writes=[B["m"]])
    kb.op(dve, "tensor_scalar", mc, m, 1e-4, None, ALU.max, reads=[B["m"]], writes=[B["mc"]])
    kb.op(dve, "tensor_scalar", sbh, mc, 1.0 / 6.0, None, ALU.mult,
          reads=[B["mc"]], writes=[B["sbh"]])
    kb.op(dve, "reciprocal", r6f, mc, reads=[B["mc"]], writes=[B["r6f"]])
    kb.op(dve, "tensor_scalar", r6h, r6f, 6.0, None, ALU.mult,
          reads=[B["r6f"]], writes=[B["r6h"]])
    bc = r6h.unsqueeze(2).broadcast_to([128, nb, 16])
    kb.op(dve, "tensor_tensor", v3(xa), v3(xa), bc, ALU.mult,
          reads=[B["xa"], B["r6h"]], writes=[B["xa"]])
    u = xa
    kb.op(dve, "tensor_scalar", qa.bitcast(I16), u, 2.0, None, ALU.mult,
          reads=[B["xa"]], writes=[B["a"]])
    kb.op(dve, "tensor_scalar", qa, qa.bitcast(I16), 4.0, 0.5, ALU.min, ALU.mult,
          reads=[B["a"]], writes=[B["a"]])
    kb.op(dve, "tensor_scalar", qb_.bitcast(I16), u, 1.0, None, ALU.mult,
          reads=[B["xa"]], writes=[B["b"]])
    kb.op(dve, "tensor_scalar", qb_, qb_.bitcast(I16), 2.0, 4.0, ALU.max, ALU.min,
          reads=[B["b"]], writes=[B["b"]])
    kb.op(dve, "tensor_scalar", qd.bitcast(I16), u, 4000.0, -20002.0, ALU.mult, ALU.add,
          reads=[B["xa"]], writes=[B["d"]])
    kb.op(dve, "tensor_scalar", qd, qd.bitcast(I16), -2.0, 0.0, ALU.max, ALU.min,
          reads=[B["d"]], writes=[B["d"]])
    kb.op(dve, "tensor_add", qa, qa, qb_, reads=[B["a"], B["b"]], writes=[B["a"]])
    kb.op(dve, "tensor_add", qa, qa, qd, reads=[B["a"], B["d"]], writes=[B["a"]])
    kb.op(act, "activation", qd, xs_ap, AF.Sign, reads=[bxs], writes=[B["d"]])
    kb.op(dve, "tensor_mul", qa, qa, qd, reads=[B["a"], B["d"]], writes=[B["a"]])
    sb_bc = sbh.unsqueeze(2).broadcast_to([128, nb, 16])
    kb.op(dve, "tensor_tensor", v3(xq_ap), v3(qa), sb_bc, ALU.mult,
          reads=[B["a"], B["sbh"]], writes=[bxq])


def build(n_seq=2, nt=16, only_pass1=False, dbg=None, no_dyadic=False, no_c0=False):
    nc = bass.Bass("TRN2", target_bir_lowering=False)
    es = ExitStack()
    kb = KB(nc, es)
    sbt = lambda name, shape, dt=BF16: es.enter_context(nc.sbuf_tensor(name, shape, dt))

    x_d = nc.dram_tensor("x", [n_seq, N, D], F32, kind="ExternalInput").ap()
    out_d = nc.dram_tensor("out", [n_seq, N, D], F32, kind="ExternalOutput").ap()
    part_d = nc.dram_tensor("part", [n_seq, N, D], F32, kind="Internal").ap()
    xnb_d = nc.dram_tensor("xnbc", [n_seq, N, D], BF16, kind="Internal").ap()
    wg_d = nc.dram_tensor("wg", [I, D], F32, kind="ExternalInput").ap()
    wu_d = nc.dram_tensor("wu", [I, D], F32, kind="ExternalInput").ap()
    wd_d = nc.dram_tensor("wd", [D, I], F32, kind="ExternalInput").ap()
    wqkv_d = nc.dram_tensor("wqkv", [3 * D, D], F32, kind="ExternalInput").ap()
    bqkv_d = nc.dram_tensor("bqkvb", [128, 3 * D], F32, kind="ExternalInput").ap()
    wgt_d = nc.dram_tensor("wgate", [D, D], F32, kind="ExternalInput").ap()
    bgt_d = nc.dram_tensor("bgateb", [128, D], F32, kind="ExternalInput").ap()
    wo_d = nc.dram_tensor("wout", [D, D], F32, kind="ExternalInput").ap()
    bo_d = nc.dram_tensor("boutb", [128, D], F32, kind="ExternalInput").ap()
    bwin_d = nc.dram_tensor("bwin", [H, WCH, 128, 128], F32, kind="ExternalInput").ap()
    bdyn_d = nc.dram_tensor("bdyn", [128, H * ND], F32, kind="ExternalInput").ap()

    B = {}
    def buf(nm):
        if nm not in B:
            B[nm] = Buf(nm)
        return B[nm]

    xnb = sbt("xnb", [128, D])
    small = [{nm: sbt(f"ln_{nm}{p}", [128, 1], F32)
              for nm in ("s1", "s2", "mean", "var", "y", "nmr", "tmp1")} for p in range(2)]
    part_t = sbt("part_t", [128, D], F32)
    ln_b = [{nm: buf(f"{nm}{p}") for nm in ("s1", "s2", "mean", "var", "y", "nmr", "tmp1", "xsq")}
            for p in range(2)]
    for p in range(2):
        buf(f"q_xa{p}")
    for nm in ("m", "mc", "r6f", "r6h", "sbh", "a", "b", "d"):
        buf(f"q_{nm}")
    b_xnb, b_part = buf("xnb"), buf("part_t")

    pe, dve, act, pool, sy = kb.pe, kb.dve, kb.act, kb.pool, kb.sy

    with ExitStack() as es1:
        s1t = lambda name, shape, dt=BF16: es1.enter_context(nc.sbuf_tensor(name, shape, dt))
        ffnw = s1t("ffnw", [128, 3 * 8 * I])
        WGO, WUO, WDO = 0, 8 * I, 16 * I
        FST = 3 * 8 * I
        wgqT = ffnw[:, WGO:WGO + 8 * I]
        wuqT = ffnw[:, WUO:WUO + 8 * I]
        wdqT = ffnw[:, WDO:WDO + 22 * D]
        b_wgq_t = [buf(f"wgqT{i}") for i in range(22)]
        b_wuq_t = [buf(f"wuqT{i}") for i in range(22)]
        b_wdq_t = [buf(f"wdqT{i}") for i in range(8)]
        x_t = [s1t(f"x_t{p}", [128, D], F32) for p in range(2)]
        xnf = [s1t(f"xnf{p}", [128, D], F32) for p in range(2)]
        b_x = [buf(f"x_t{p}") for p in range(2)]
        b_xnf = [buf(f"xnf{p}") for p in range(2)]
        for p in range(2):
            small[p]["xsq"] = xnf[p]
            ln_b[p]["xsq"] = b_xnf[p]
        qshared = {}
        for nm in ("qa", "qb", "qd"):
            qshared[nm] = s1t(nm, [128, QW], F16)
        for nm in ("qm", "qr6f"):
            qshared[nm] = s1t(nm, [128, QW // 16], F32)
        qshared["qmc"] = qshared["qm"]
        for nm in ("qr6h", "qsbh"):
            qshared[nm] = s1t(nm, [128, QW // 16], F16)
        q_t = []
        for p in range(2):
            d = dict(qshared)
            d["qxa"] = s1t(f"qxa{p}", [128, QW], F16)
            q_t.append(d)
        xq1 = s1t("xq", [128, D])
        xq = [xq1, xq1]
        xq_fm = [s1t(f"xq_fm{p}", [128, D]) for p in range(2)]
        hq = [s1t(f"hq{p}", [128, I]) for p in range(2)]
        hq_fm = [s1t(f"hq_fm{p}", [128, I]) for p in range(2)]
        hsg1 = s1t("h_sg", [128, 512], F16)
        h_sg = [hsg1, hsg1]
        h_gt = [s1t(f"h_gt{p}", [128, 512], F16) for p in range(2)]
        hup1 = s1t("h_up", [128, 512], F16)
        h_up = [hup1, hup1]
        bias_out = s1t("bias_out", [128, D])
        b_xq1 = buf("xq")
        b_xq = [b_xq1, b_xq1]
        b_xqfm = [buf(f"xq_fm{p}") for p in range(2)]
        b_hq = [buf(f"hq{p}") for p in range(2)]
        b_hqfm = [buf(f"hq_fm{p}") for p in range(2)]
        b_hsg1 = buf("h_sg")
        b_hsg = [b_hsg1, b_hsg1]
        b_hgt = [buf(f"h_gt{p}") for p in range(2)]
        b_hup1 = buf("h_up")
        b_hup = [b_hup1, b_hup1]
        b_bo = buf("bias_out")
        xnbp = [xnb, xnb]
        b_xnbp = [b_xnb, b_xnb]
        part_p = [part_t, s1t("part2", [128, D], F32)]
        b_partp = [b_part, buf("part2")]
        wstage = x_t[0]
        b_ws = b_x[0]
        kb.dma(sy, xnf[0][:], bo_d, writes=[b_xnf[0]])
        kb.op(act, "copy", bias_out[:], xnf[0][:], reads=[b_xnf[0]], writes=[b_bo])
        wjobs = []
        for i in range(22):
            wjobs.append((wg_d, WGO, b_wgq_t[i], i, D, I))
            wjobs.append((wu_d, WUO, b_wuq_t[i], i, D, I))
        for i in range(8):
            wjobs.append((wd_d, WDO, b_wdq_t[i], i, I, D))
        for wj, (w_d, wTO, bwT, it, tw, ost) in enumerate(wjobs):
            wst, bst = x_t[wj % 2], b_x[wj % 2]
            hqt, bhqt = hq[wj % 2], b_hq[wj % 2]
            for poff in range(0, tw, D):
                cw = min(D, tw - poff)
                kb.dma(sy, wst[:, 0:cw], w_d[it * 128:(it + 1) * 128, poff:poff + cw],
                       writes=[bst])
                emit_quant(kb, q_t[wj % 2], B, wst[:, 0:cw], hqt[:, poff:poff + cw],
                           bst, bhqt, cw, par=wj % 2)
            nblk = tw // 128
            dst = bass.AP(ffnw, wTO + it * 128,
                          [[FST, 128], [ost, nblk], [1, 128]])
            src = hqt[:, 0:tw].rearrange("p (c q) -> p c q", c=nblk)
            kb.dma(act, dst, src, transpose=True, reads=[bhqt], writes=[bwT])
        pg = [es1.enter_context(nc.psum_tensor(f"pg{p}", [128, 512], F32)) for p in range(2)]
        pu = [es1.enter_context(nc.psum_tensor(f"pu{p}", [128, 512], F32)) for p in range(2)]
        pd = [es1.enter_context(nc.psum_tensor(f"pd{p}", [128, D], F32)) for p in range(2)]
        b_pg = [buf(f"pg{p}") for p in range(2)]
        b_pu = [buf(f"pu{p}") for p in range(2)]
        b_pd = [buf(f"pd{p}") for p in range(2)]

        tiles = [(it, sq) for it in range(nt) for sq in range(n_seq)]

        def p1_s0(ti):
            it, sq = tiles[ti]
            tp = ti % 2
            row = slice(it * 128, (it + 1) * 128)
            xt, bx = x_t[tp], b_x[tp]
            xnfp, bxnf = xnf[tp], b_xnf[tp]
            xqt, bxq = xq[tp], b_xq[tp]
            xqf, bxqf = xq_fm[tp], b_xqfm[tp]
            kb.dma(sy, xt[:], x_d[sq, row, :], writes=[bx])
            emit_ln(kb, small[tp], ln_b[tp], xt[:], None, xnfp[:], bx, None, bxnf)
            kb.dma(pool, xnb_d[sq, row, :], xnfp[:], reads=[bxnf], writes=[buf("xnb_d")])
            emit_quant(kb, q_t[tp], B, xnfp[:], xqt[:], bxnf, bxq, D, par=tp)
            kb.dma(act, xqf[:].rearrange("p (c q) -> p c q", c=8),
                   xqt[:].rearrange("p (c q) -> p c q", c=8),
                   transpose=True, reads=[bxq], writes=[bxqf])

        def p1_s1(ti):
            it, sq = tiles[ti]
            tp = ti % 2
            xqf, bxqf = xq_fm[tp], b_xqfm[tp]
            hqt, bhq = hq[tp], b_hq[tp]
            hqf, bhqf = hq_fm[tp], b_hqfm[tp]
            hfl_ap = hqf[:].bitcast(F16)
            for ci, (off, cw) in enumerate(ICHUNKS):
                cp = ci % 2
                for dc in range(8):
                    lhsT = xqf[:, dc * 128:(dc + 1) * 128]
                    wtl = list(range(off // 128, (off + cw + 127) // 128))
                    kb.op(pe, "matmul", pg[cp][:, 0:cw], lhsT,
                          wgqT[:, dc * I + off: dc * I + off + cw],
                          start=(dc == 0), stop=(dc == 7),
                          reads=[bxqf] + [b_wgq_t[i] for i in wtl], writes=[b_pg[cp]])
                    kb.op(pe, "matmul", pu[cp][:, 0:cw], lhsT,
                          wuqT[:, dc * I + off: dc * I + off + cw],
                          start=(dc == 0), stop=(dc == 7),
                          reads=[bxqf] + [b_wuq_t[i] for i in wtl], writes=[b_pu[cp]])
                kb.op(act, "activation", h_sg[tp][:, 0:cw], pg[cp][:, 0:cw], AF.Silu,
                      reads=[b_pg[cp]], writes=[b_hsg[tp]])
                kb.op(act, "copy", h_up[tp][:, 0:cw], pu[cp][:, 0:cw],
                      reads=[b_pu[cp]], writes=[b_hup[tp]])
                kb.op(dve, "tensor_mul", hfl_ap[:, off:off + cw], h_sg[tp][:, 0:cw],
                      h_up[tp][:, 0:cw], reads=[b_hsg[tp], b_hup[tp]], writes=[bhqf])
            for qoff in range(0, I, QW):
                cw = min(QW, I - qoff)
                emit_quant(kb, q_t[tp], B, hfl_ap[:, qoff:qoff + cw],
                           hqt[:, qoff:qoff + cw], bhqf, bhq, cw, par=tp)
            kb.dma(act, hqf[:].rearrange("p (c q) -> p c q", c=22),
                   hqt[:].rearrange("p (c q) -> p c q", c=22),
                   transpose=True, reads=[bhq], writes=[bhqf])

        def p1_s2(ti):
            it, sq = tiles[ti]
            tp = ti % 2
            row = slice(it * 128, (it + 1) * 128)
            xt, bx = x_t[tp], b_x[tp]
            hqf, bhqf = hq_fm[tp], b_hqfm[tp]
            ptt, bpt = part_p[tp], b_partp[tp]
            for half in range(2):
                for ic in range(22):
                    kb.op(pe, "matmul", pd[tp][:, half * 512:(half + 1) * 512],
                          hqf[:, ic * 128:(ic + 1) * 128],
                          wdqT[:, ic * D + half * 512: ic * D + (half + 1) * 512],
                          start=(ic == 0), stop=(ic == 21),
                          reads=[bhqf] + [b_wdq_t[i] for i in range(half * 4, half * 4 + 4)], writes=[b_pd[tp]])
            kb.op(dve, "tensor_add", ptt[:], pd[tp][:], xt[:],
                  reads=[b_pd[tp], bx], writes=[bpt])
            kb.op(dve, "tensor_add", ptt[:], ptt[:], bias_out[:],
                  reads=[bpt, b_bo], writes=[bpt])
            dst = (out_d if only_pass1 else part_d)[sq, row, :]
            kb.dma(sy, dst, ptt[:], reads=[bpt],
                   writes=([] if only_pass1 else [buf("part_d")]))

        T = len(tiles)
        for k in range(T + 2):
            if 0 <= k - 2 < T:
                p1_s2(k - 2)
            if 0 <= k - 1 < T:
                p1_s1(k - 1)
            if k < T:
                p1_s0(k)

    if only_pass1:
        kb.finalize()
        es.close()
        return nc

    env = dict(locals()); env['sbt'] = sbt; env['dbg'] = dbg
    env['no_dyadic'] = no_dyadic; env['no_c0'] = no_c0
    build_pass2(nc, es, kb, B, buf, env)
    kb.finalize()
    es.close()
    return nc


def build_pass2(nc, es, kb, B, buf, env):
    sbt = env["sbt"]
    small = env["small"]
    xnb, part_t = env["xnb"], env["part_t"]
    x_d, out_d, part_d, xnb_d = env["x_d"], env["out_d"], env["part_d"], env["xnb_d"]
    wqkv_d, bqkv_d = env["wqkv_d"], env["bqkv_d"]
    wgt_d, bgt_d, wo_d = env["wgt_d"], env["bgt_d"], env["wo_d"]
    bwin_d, bdyn_d = env["bwin_d"], env["bdyn_d"]
    n_seq, nt = env["n_seq"], env["nt"]
    no_dyadic = env.get("no_dyadic"); no_c0 = env.get("no_c0")
    pe, dve, act, pool, sy = kb.pe, kb.dve, kb.act, kb.pool, kb.sy
    b_partd, b_xnbd = buf("part_d"), buf("xnb_d")
    NEG = -1e30

    with ExitStack() as es2:
        s2t = lambda name, shape, dt=BF16: es2.enter_context(nc.sbuf_tensor(name, shape, dt))
        attnw = s2t("attnw", [128, 8 * 3 * D + 16 * D])
        ATST = 8 * 3 * D + 16 * D
        WQO, WGO2, WOO = 0, 8 * 3 * D, 8 * 3 * D + 8 * D
        wqkvT = attnw[:, WQO:WQO + 24 * D]
        wgT = attnw[:, WGO2:WGO2 + 8 * D]
        woT = attnw[:, WOO:WOO + 8 * D]
        b_wq, b_wg2, b_wo = buf("wqkvT"), buf("wgT"), buf("woT")
        k_tm = s2t("k_tm", [128, 16 * D])
        v_tm = s2t("v_tm", [128, 16 * D])
        kfm = s2t("kfm", [128, 3 * D])
        ebw = s2t("ebw", [128, H * WCH * 128])
        ebd = s2t("ebd", [128, H * ND])
        bias_qkv = s2t("bias_qkv", [128, 3 * D])
        bias_gate = s2t("bias_gate", [128, D])
        ones_col = s2t("ones_col", [128, 1])
        qtm1 = s2t("q_tm", [128, D])
        q_tm = [qtm1, qtm1]
        q_fm = [s2t(f"q_fm{p}", [128, D]) for p in range(2)]
        e4 = [s2t(f"e4{p}", [128, 4 * WCH * 128]) for p in range(2)]
        sdyn = s2t("sdyn", [128, H * ND], F32)
        edyn = s2t("edyn", [128, H * ND])
        dsum = s2t("dsum", [128, H], F32)
        ssum = [s2t(f"ssum{p}", [128, H], F32) for p in range(2)]
        srec = [s2t(f"srec{p}", [128, H], F32) for p in range(2)]
        g1 = s2t("g1", [128, D])
        g21 = s2t("g2", [128, D])
        g2 = [g21, g21]
        g41 = s2t("g4", [128, D])
        g4 = [g41, g41]
        flat_sb = [s2t(f"flat_sb{p}", [128, D]) for p in range(2)]
        fg1t = s2t("fg", [128, D])
        fg = [fg1t, fg1t]
        fgf1 = s2t("fg_fm", [128, D])
        fg_fm = [fgf1, fgf1]
        xnb2p = [xnb, xnb]
        xnbfm2 = [s2t(f"xnbfm2{p}", [128, D]) for p in range(2)]
        part2p = [part_t, part_t]
        b_part2p = [B["part_t"], B["part_t"]]
        wstage2 = part_t
        b_ws2 = B["part_t"]
        b_ktm = [buf(f"k_tm{j}") for j in range(16)]
        b_vtm = [buf(f"v_tm{j}") for j in range(16)]
        b_kfm = [buf(f"kfm{j}") for j in range(3)]
        b_ebw, b_ebd = buf("ebw"), buf("ebd")
        b_bq, b_bg = buf("bias_qkv"), buf("bias_gate")
        b_ones = buf("ones_col")
        b_qtm1 = buf("q_tm")
        b_qtm = [b_qtm1, b_qtm1]
        b_qfm = [buf(f"q_fm{p}") for p in range(2)]
        b_e4 = [buf(f"e4{p}") for p in range(2)]
        b_sdyn, b_edyn = buf("sdyn"), buf("edyn")
        b_dsum = buf("dsum")
        b_ssum = [buf(f"ssum{p}") for p in range(2)]
        b_srec = [buf(f"srec{p}") for p in range(2)]
        b_g1 = buf("g1")
        b_g21 = buf("g2")
        b_g2 = [b_g21, b_g21]
        b_g41 = buf("g4")
        b_g4 = [b_g41, b_g41]
        b_flat = [buf(f"flat_sb{p}") for p in range(2)]
        b_fg1 = buf("fg")
        b_fg = [b_fg1, b_fg1]
        b_fgf1 = buf("fg_fm")
        b_fgfm = [b_fgf1, b_fgf1]
        b_xnb2p = [B["xnb"], B["xnb"]]
        b_xnbfm2 = [buf(f"xnbfm2{p}") for p in range(2)]

        P1 = es2.enter_context(nc.psum_tensor("P1", [128, 2048], F32))
        P2 = es2.enter_context(nc.psum_tensor("P2", [128, D], F32))
        P3 = es2.enter_context(nc.psum_tensor("P3", [128, 512], F32))
        P4 = es2.enter_context(nc.psum_tensor("P4", [128, 512], F32))
        b_P1b = [buf(f"P1b{b}") for b in range(4)]
        b_P2, b_P3, b_P4 = buf("P2"), buf("P3"), buf("P4")
        qkv_slots = [(P1[:, 0:512], b_P1b[0]), (P1[:, 512:1024], b_P1b[1]),
                     (P4[:, 0:512], b_P4)]

        kb.op(dve, "memset", ones_col[:], 1.0, writes=[b_ones])
        kb.op(dve, "memset", P1[:], 0.0, writes=b_P1b)
        kb.op(dve, "memset", P2[:], 0.0, writes=[b_P2])
        kb.op(dve, "memset", P3[:], 0.0, writes=[b_P3])
        kb.op(dve, "memset", P4[:], 0.0, writes=[b_P4])
        wcv = [(xnb, B["xnb"]), (q_tm[0], b_qtm[0])]
        for it in range(24):
            cv, bcv = wcv[it % 2]
            if it < 8:
                kb.dma(sy, wstage2[:], wqkv_d[it * 128:(it + 1) * 128, :], writes=[b_ws2])
                kb.op(act, "activation", cv[:], wstage2[:], AF.Identity, scale=0.125,
                      reads=[b_ws2], writes=[bcv])
            else:
                kb.dma(pool, cv[:], wqkv_d[it * 128:(it + 1) * 128, :], writes=[bcv])
            dst = bass.AP(attnw, WQO + it * 128, [[ATST, 128], [3 * D, 8], [1, 128]])
            kb.dma(act, dst, cv[:].rearrange("p (c q) -> p c q", c=8),
                   transpose=True, reads=[bcv], writes=[b_wq])
        for wi, (w_d, base, bw) in enumerate(((wgt_d, WGO2, b_wg2), (wo_d, WOO, b_wo))):
            for it in range(8):
                cv, bcv = wcv[it % 2]
                kb.dma(pool, cv[:], w_d[it * 128:(it + 1) * 128, :], writes=[bcv])
                dst = bass.AP(attnw, base + it * 128, [[ATST, 128], [D, 8], [1, 128]])
                kb.dma(act, dst, cv[:].rearrange("p (c q) -> p c q", c=8),
                       transpose=True, reads=[bcv], writes=[bw])
        for c in range(3):
            kb.dma(sy, wstage2[:, 0:D], bqkv_d[:, c * D:(c + 1) * D], writes=[b_ws2])
            kb.op(act, "activation", bias_qkv[:, c * D:(c + 1) * D], wstage2[:, 0:D],
                  AF.Identity, scale=(0.125 if c == 0 else 1.0),
                  reads=[b_ws2], writes=[b_bq])
        kb.dma(sy, wstage2[:, 0:D], bgt_d, writes=[b_ws2])
        kb.op(act, "copy", bias_gate[:], wstage2[:, 0:D], reads=[b_ws2], writes=[b_bg])
        for ch in range(6):
            src = bass.AP(bwin_d.tensor, ch * 8 * 128 * 128,
                          [[128, 128], [128 * 128, 8], [1, 128]])
            kb.dma(sy, wstage2[:, 0:1024], src, writes=[b_ws2])
            kb.op(act, "activation", ebw[:, ch * 1024:(ch + 1) * 1024],
                  wstage2[:, 0:1024], AF.Exp, reads=[b_ws2], writes=[b_ebw])
        kb.dma(sy, wstage2[:, 0:H * ND], bdyn_d, writes=[b_ws2])
        kb.op(act, "activation", ebd[:], wstage2[:, 0:H * ND], AF.Exp,
              reads=[b_ws2], writes=[b_ebd])

        tiles2 = [(sq, it) for sq in range(n_seq) for it in range(nt)]
        qkv_slots = [(P3[:, 0:512], b_P3), (P4[:, 0:512], b_P4)]

        def p2_t0(ti):
            sq, it = tiles2[ti]
            tp = ti % 2
            row = slice(it * 128, (it + 1) * 128)
            xnbt, bxnbt = xnb2p[tp], b_xnb2p[tp]
            xfm, bxfm = xnbfm2[tp], b_xnbfm2[tp]
            qtm, bqtm = q_tm[tp], b_qtm[tp]
            qfm, bqfm = q_fm[tp], b_qfm[tp]
            kb.dma(sy, xnbt[:], xnb_d[sq, row, :], reads=[b_xnbd], writes=[bxnbt])
            kb.dma(act, xfm[:].rearrange("p (c q) -> p c q", c=8),
                   xnbt[:].rearrange("p (c q) -> p c q", c=8),
                   transpose=True, reads=[bxnbt], writes=[bxfm])
            for g in range(6):
                ps_g, b_ps = qkv_slots[g % 2]
                for dc in range(8):
                    kb.op(pe, "matmul", ps_g, xfm[:, dc * 128:(dc + 1) * 128],
                          wqkvT[:, dc * 3 * D + g * 512: dc * 3 * D + (g + 1) * 512],
                          start=(dc == 0), stop=(dc == 7),
                          reads=[bxfm, b_wq], writes=[b_ps])
                if g < 2:
                    dstap, dbuf = qtm[:, g * 512:(g + 1) * 512], bqtm
                elif g < 4:
                    dstap = k_tm[:, it * D + (g - 2) * 512: it * D + (g - 1) * 512]
                    dbuf = b_ktm[it]
                else:
                    dstap = v_tm[:, it * D + (g - 4) * 512: it * D + (g - 3) * 512]
                    dbuf = b_vtm[it]
                kb.op(dve, "scalar_tensor_tensor", dstap, ps_g, 1.0,
                      bias_qkv[:, g * 512:(g + 1) * 512], ALU.mult, ALU.add,
                      reads=[b_ps, b_bq], writes=[dbuf])
            kb.dma(act, qfm[:].rearrange("p (c q) -> p c q", c=8),
                   qtm[:].rearrange("p (c q) -> p c q", c=8),
                   transpose=True, reads=[bqtm], writes=[bqfm])
            kb.dma(act, kfm[:, (it % 3) * D:(it % 3 + 1) * D].rearrange("p (c q) -> p c q", c=8),
                   k_tm[:, it * D:(it + 1) * D].rearrange("p (c q) -> p c q", c=8),
                   transpose=True, reads=[b_ktm[it]], writes=[b_kfm[it % 3]])

        def p2_t1(ti):
            sq, it = tiles2[ti]
            tp = ti % 2
            qtm, bqtm = q_tm[tp], b_qtm[tp]
            qfm, bqfm = q_fm[tp], b_qfm[tp]
            kb.op(dve, "memset", sdyn[:], NEG, writes=[b_sdyn])
            for mi, o in enumerate(DYADIC):
                ot = o // 128
                if it < ot or no_dyadic:
                    continue
                kb.op(dve, "tensor_mul", g1[:], qtm[:],
                      k_tm[:, (it - ot) * D:(it - ot + 1) * D],
                      reads=[bqtm, b_ktm[it - ot]], writes=[b_g1])
                kb.op(dve, "reduce_sum", sdyn[:, mi * H:(mi + 1) * H],
                      g1[:].rearrange("p (h d) -> p h d", d=HD), AX.X,
                      reads=[b_g1], writes=[b_sdyn])
            kb.op(act, "activation", edyn[:], sdyn[:], AF.Exp,
                  reads=[b_sdyn], writes=[b_edyn])
            kb.op(dve, "tensor_mul", edyn[:], edyn[:], ebd[:],
                  reads=[b_edyn, b_ebd], writes=[b_edyn])
            ev = bass.AP(edyn, 0, [[edyn[:].ap[0][0], 128], [1, H], [H, ND]])
            kb.op(dve, "reduce_sum", dsum[:], ev, AX.X,
                  reads=[b_edyn], writes=[b_dsum])
            for hg in range(4):
                ep = hg % 2
                heads = range(4 * hg, 4 * hg + 4)
                for hh, h in sorted(enumerate(heads), key=lambda p: p[1] % 2):
                    for c in range(WCH):
                        jt = it - 2 + c
                        reg = P1[:, hh * 512 + c * 128: hh * 512 + (c + 1) * 128]
                        if jt < 0 or (no_c0 and c == 0):
                            kb.op(dve, "memset", reg, NEG, writes=[b_P1b[hh]])
                            continue
                        pb = (h % 2) * 64
                        lhsT = kfm[pb:pb + 64,
                                   (jt % 3) * D + (h // 2) * 128:(jt % 3) * D + (h // 2) * 128 + 128]
                        rhs = qfm[pb:pb + 64, (h // 2) * 128:(h // 2) * 128 + 128]
                        kb.op(pe, "matmul", reg, lhsT, rhs, start=True, stop=True,
                              reads=[b_kfm[jt % 3], bqfm], writes=[b_P1b[hh]])
                pv = P1[:].rearrange("p (a r) -> p a r", r=512)[:, :, 0:WCH * 128]
                ev4 = e4[ep][:].rearrange("p (a r) -> p a r", r=WCH * 128)
                kb.op(act, "activation", ev4, pv, AF.Exp,
                      reads=b_P1b, writes=[b_e4[ep]])
                ebv = ebw[:, (4 * hg) * WCH * 128:(4 * hg + 4) * WCH * 128]
                kb.op(dve, "tensor_mul", e4[ep][:], e4[ep][:], ebv,
                      reads=[b_e4[ep], b_ebw], writes=[b_e4[ep]])
                for hh, h in enumerate(heads):
                    valid = [c for c in range(WCH)
                             if it - 2 + c >= 0 and not (no_c0 and c == 0)]
                    for vi, c in enumerate(valid):
                        jt = it - 2 + c
                        ech = e4[ep][:, hh * 384 + c * 128: hh * 384 + (c + 1) * 128]
                        kb.op(pe, "matmul", P1[:, 1920 + h: 1921 + h], ech, ones_col[:],
                              start=(vi == 0), stop=(vi == len(valid) - 1),
                              skip_group_check=True,
                              reads=[b_e4[ep], b_ones], writes=[b_P1b[3]])
                        kb.op(pe, "matmul", P2[:, h * HD:(h + 1) * HD], ech,
                              v_tm[:, jt * D + h * HD: jt * D + (h + 1) * HD],
                              start=(vi == 0), stop=(vi == len(valid) - 1),
                              skip_group_check=True,
                              reads=[b_e4[ep], b_vtm[jt]], writes=[b_P2])
            kb.op(dve, "tensor_add", ssum[tp][:], P1[:, 1920:1920 + H], dsum[:],
                  reads=[b_P1b[3], b_dsum], writes=[b_ssum[tp]])
            kb.op(dve, "reciprocal", srec[tp][:], ssum[tp][:],
                  reads=[b_ssum[tp]], writes=[b_srec[tp]])
            vd = [] if no_dyadic else [o for o in DYADIC if it >= o // 128]
            if not vd:
                kb.op(dve, "memset", g2[tp][:], 0.0, writes=[b_g2[tp]])
            for vi, o in enumerate(vd):
                mi = DYADIC.index(o)
                e_b = edyn[:, mi * H:(mi + 1) * H]
                e_bc = e_b.unsqueeze(2).broadcast_to([128, H, HD])
                tgt, tb = (g2[tp], b_g2[tp]) if vi == 0 else (g1, b_g1)
                kb.op(dve, "tensor_tensor",
                      tgt[:].rearrange("p (h d) -> p h d", d=HD),
                      v_tm[:, (it - o // 128) * D:(it - o // 128 + 1) * D]
                      .rearrange("p (h d) -> p h d", d=HD),
                      e_bc, ALU.mult,
                      reads=[b_vtm[it - o // 128], b_edyn], writes=[tb])
                if vi > 0:
                    kb.op(dve, "tensor_add", g2[tp][:], g2[tp][:], g1[:],
                          reads=[b_g2[tp], b_g1], writes=[b_g2[tp]])
            srec_bc = srec[tp][:].unsqueeze(2).broadcast_to([128, H, HD])
            v3h = lambda ap: ap.rearrange("p (h d) -> p h d", d=HD)
            kb.op(dve, "tensor_tensor", v3h(flat_sb[tp][:]), v3h(P2[:]), srec_bc,
                  ALU.mult, reads=[b_P2, b_srec[tp]], writes=[b_flat[tp]])
            kb.op(dve, "tensor_tensor", v3h(g2[tp][:]), v3h(g2[tp][:]), srec_bc,
                  ALU.mult, reads=[b_g2[tp], b_srec[tp]], writes=[b_g2[tp]])
            kb.op(dve, "tensor_add", flat_sb[tp][:], flat_sb[tp][:], g2[tp][:],
                  reads=[b_flat[tp], b_g2[tp]], writes=[b_flat[tp]])

        def p2_t2(ti):
            sq, it = tiles2[ti]
            tp = ti % 2
            row = slice(it * 128, (it + 1) * 128)
            xfm, bxfm = xnbfm2[tp], b_xnbfm2[tp]
            ptt, bpt = part2p[tp], b_part2p[tp]
            kb.dma(sy, ptt[:], part_d[sq, row, :], reads=[b_partd], writes=[bpt])
            for g in range(2):
                for dc in range(8):
                    kb.op(pe, "matmul", P1[:, g * 512:(g + 1) * 512],
                          xfm[:, dc * 128:(dc + 1) * 128],
                          wgT[:, dc * D + g * 512: dc * D + (g + 1) * 512],
                          start=(dc == 0), stop=(dc == 7),
                          reads=[bxfm, b_wg2], writes=[b_P1b[g]])
            kb.op(dve, "tensor_add", g4[tp][:], P1[:, 0:D], bias_gate[:],
                  reads=[b_P1b[0], b_P1b[1], b_bg], writes=[b_g4[tp]])
            kb.op(act, "activation", g4[tp][:], g4[tp][:], AF.Tanh, scale=0.5,
                  reads=[b_g4[tp]], writes=[b_g4[tp]])
            kb.op(dve, "tensor_scalar", g4[tp][:], g4[tp][:], 0.5, 0.5,
                  ALU.mult, ALU.add, reads=[b_g4[tp]], writes=[b_g4[tp]])
            kb.op(dve, "tensor_mul", fg[tp][:], flat_sb[tp][:], g4[tp][:],
                  reads=[b_flat[tp], b_g4[tp]], writes=[b_fg[tp]])
            kb.dma(act, fg_fm[tp][:].rearrange("p (c q) -> p c q", c=8),
                   fg[tp][:].rearrange("p (c q) -> p c q", c=8),
                   transpose=True, reads=[b_fg[tp]], writes=[b_fgfm[tp]])
            for g in range(2):
                for dc in range(8):
                    kb.op(pe, "matmul", P2[:, g * 512:(g + 1) * 512],
                          fg_fm[tp][:, dc * 128:(dc + 1) * 128],
                          woT[:, dc * D + g * 512: dc * D + (g + 1) * 512],
                          start=(dc == 0), stop=(dc == 7),
                          reads=[b_fgfm[tp], b_wo], writes=[b_P2])
            kb.op(dve, "tensor_add", ptt[:], P2[:], ptt[:],
                  reads=[b_P2, bpt], writes=[bpt])
            kb.dma(sy, out_d[sq, row, :], ptt[:], reads=[bpt])

        T2n = len(tiles2)
        for k in range(T2n + 2):
            if 0 <= k - 2 < T2n:
                p2_t2(k - 2)
            if 0 <= k - 1 < T2n:
                p2_t1(k - 1)
            if k < T2n:
                p2_t0(k)


_NC_CACHE = {}


def kernel(**inputs):
    maps = core_inputs(inputs, n_cores=8)
    if "nc" not in _NC_CACHE:
        _NC_CACHE["nc"] = build(n_seq=2, nt=16, only_pass1=False)
    nc = _NC_CACHE["nc"]
    res = run_bass_kernel_spmd(nc, maps, core_ids=list(range(8)))
    out = np.concatenate([np.asarray(r["out"]) for r in res.results], axis=0)
    return out.astype(np.float32)
